# version 44
# baseline (speedup 1.0000x reference)
import contextlib

import numpy as np
import ml_dtypes

import concourse.bass as bass
import concourse.mybir as mybir
from concourse.bass_utils import run_bass_kernel_spmd

F32 = mybir.dt.float32
BF16 = mybir.dt.bfloat16
AF = mybir.ActivationFunctionType
ALU = mybir.AluOpType
AX = mybir.AxisListType.X

D = 1024
T = 4096
NSLOT = 16
NCHUNK = 8
NTILE = 32
GS = 4
EPS = 1e-5
MASKVAL = -30000.0

ENGINES = ("sync", "tensor", "vector", "scalar", "gpsimd")
ESEM = {"tensor": "pe", "vector": "dve", "scalar": "act", "gpsimd": "pool",
        "sync": "spe"}
QPOOL = {"sync": [f"qs{i}" for i in range(8)],
         "scalar": [f"qc{i}" for i in range(4)]}
ALLSEMS = ["pe", "dve", "act", "pool", "spe"] + QPOOL["sync"] + QPOOL["scalar"]


class Op:
    __slots__ = ("oid", "engine", "fns", "dma", "cost", "act_set", "preds",
                 "succs", "prio", "start", "finish", "semn", "semv", "bonus")

    def __init__(self, oid, engine, fns, dma, cost, act_set):
        self.oid = oid
        self.engine = engine
        self.fns = fns
        self.dma = dma
        self.cost = cost
        self.act_set = act_set
        self.preds = set()
        self.succs = []
        self.prio = 0.0
        self.start = 0.0
        self.finish = 0.0
        self.semn = None
        self.semv = 0
        self.bonus = 0.0


ACT_SWITCH_NS = 1400.0


class P:

    def __init__(self):
        self.ops = []
        self.lastw = {}
        self.readers = {}
        self.default_bonus = 0.0

    def op(self, engine, fns, reads=(), writes=(), dma=False, cost=150.0,
           act_set=None, prio_bonus=0.0):
        if callable(fns):
            fns = [fns]
        o = Op(len(self.ops), engine, fns, dma, float(cost), act_set)
        o.bonus = float(prio_bonus) if prio_bonus else self.default_bonus
        for t in reads:
            w = self.lastw.get(t)
            if w is not None:
                o.preds.add(w)
        for t in writes:
            w = self.lastw.get(t)
            if w is not None:
                o.preds.add(w)
            for r in self.readers.get(t, ()):
                o.preds.add(r)
        o.preds.discard(o.oid)
        for t in writes:
            self.lastw[t] = o.oid
            self.readers[t] = []
        for t in reads:
            self.readers.setdefault(t, []).append(o.oid)
        self.ops.append(o)
        return o.oid

    def schedule(self):
        import heapq
        ops = self.ops
        for o in ops:
            for pid in o.preds:
                ops[pid].succs.append(o.oid)

        def blevel(scale):
            prio = [0.0] * len(ops)
            for o in reversed(ops):
                m = 0.0
                for s in o.succs:
                    if prio[s] > m:
                        m = prio[s]
                prio[o.oid] = o.cost * scale.get(o.engine, 1.0) + o.bonus + m
            return prio

        def greedy(prio, simscale):
            indeg = [len(o.preds) for o in ops]
            ready = {e: [] for e in ENGINES}
            data_ready = [0.0] * len(ops)
            finish = [0.0] * len(ops)
            for o in ops:
                if indeg[o.oid] == 0:
                    heapq.heappush(ready[o.engine], (-prio[o.oid], o.oid))
            engine_free = {e: 0.0 for e in ENGINES}
            queue_free = {"sync": 0.0, "scalar": 0.0}
            act_cur = None
            sched = {e: [] for e in ENGINES}
            n_left = len(ops)
            while n_left:
                best = None
                for e in ENGINES:
                    if not ready[e]:
                        continue
                    nprio, oid = ready[e][0]
                    o = ops[oid]
                    st = max(engine_free[e], data_ready[oid])
                    if o.dma:
                        st = max(st, queue_free[e])
                    if e == "scalar" and o.act_set and act_cur and \
                            o.act_set != act_cur:
                        st += ACT_SWITCH_NS
                    if best is None or st < best[0] or \
                            (st == best[0] and nprio < best[1]):
                        best = (st, nprio, e, oid)
                st, _, e, oid = best
                heapq.heappop(ready[e])
                o = ops[oid]
                finish[oid] = st + o.cost * simscale.get(e, 1.0)
                engine_free[e] = finish[oid] if not o.dma else st + 100.0
                if o.dma:
                    queue_free[e] = finish[oid]
                if e == "scalar" and o.act_set:
                    act_cur = o.act_set
                sched[e].append(oid)
                n_left -= 1
                for s in o.succs:
                    if finish[oid] > data_ready[s]:
                        data_ready[s] = finish[oid]
                    indeg[s] -= 1
                    if indeg[s] == 0:
                        heapq.heappush(ready[ops[s].engine], (-prio[s], s))
            return max(finish), sched

        est, sched = greedy(blevel({"vector": 2.5, "scalar": 1.35}),
                            {"vector": 1.25})
        self.sched = sched
        return est

    def emit(self, nc, sems):
        est = self.schedule()
        ops = self.ops
        cnt = {s: 0 for s in ALLSEMS}
        qidx = {"sync": 0, "scalar": 0}
        for e in ENGINES:
            for oid in self.sched[e]:
                o = ops[oid]
                if o.dma:
                    pool = QPOOL[e]
                    o.semn = pool[qidx[e] % len(pool)]
                    qidx[e] += 1
                    cnt[o.semn] += 16
                else:
                    o.semn = ESEM[e]
                    cnt[o.semn] += 1
                o.semv = cnt[o.semn]
        totals = dict(cnt)
        with nc.Block() as block:
            for ename in ENGINES:
                def section(eng, ename=ename):
                    clk = {}
                    for oid in self.sched[ename]:
                        o = ops[oid]
                        waits = {}
                        for pid in o.preds:
                            p_ = ops[pid]
                            if clk.get(p_.semn, 0) < p_.semv:
                                if waits.get(p_.semn, 0) < p_.semv:
                                    waits[p_.semn] = p_.semv
                        if o.dma and o.semv > 16:
                            prev = o.semv - 16
                            if clk.get(o.semn, 0) < prev and \
                                    waits.get(o.semn, 0) < prev:
                                waits[o.semn] = prev
                        for s, v in sorted(waits.items()):
                            eng.wait_ge(sems[s], v)
                            if clk.get(s, 0) < v:
                                clk[s] = v
                        inst = None
                        for fn in o.fns:
                            inst = fn(eng)
                        if inst is not None:
                            inst.then_inc(sems[o.semn], 16 if o.dma else 1)
                    if ename == "gpsimd":
                        for s in QPOOL["sync"] + QPOOL["scalar"]:
                            if totals.get(s, 0) > 0:
                                eng.wait_ge(sems[s], totals[s])
                getattr(block, ename)(section)
        return est


def build_program():
    nc = bass.Bass("TRN2", target_bir_lowering=False)

    xd = nc.dram_tensor("x", [T, D], F32, kind="ExternalInput").ap()
    wtd = nc.dram_tensor("wt", [D, D], F32, kind="ExternalInput").ap()
    utd = nc.dram_tensor("ut", [D, 128], F32, kind="ExternalInput").ap()
    vpd = nc.dram_tensor("vp", [128, D], F32, kind="ExternalInput").ap()
    gwd = nc.dram_tensor("gw", [8, D], F32, kind="ExternalInput").ap()
    emad = nc.dram_tensor("ema", [1, D], F32, kind="ExternalInput").ap()
    scald = nc.dram_tensor("scal", [1, 4], F32, kind="ExternalInput").ap()
    mmd = nc.dram_tensor("maskm", [128, 128], BF16, kind="ExternalInput").ap()
    mod = nc.dram_tensor("masko", [128, 128], BF16, kind="ExternalInput").ap()
    wmd = nc.dram_tensor("wm16", [128, NSLOT], F32, kind="ExternalInput").ap()
    idd = nc.dram_tensor("ident", [128, 128], BF16, kind="ExternalInput").ap()
    yd = nc.dram_tensor("y", [NSLOT * 128, D], F32, kind="ExternalOutput").ap()

    ex = contextlib.ExitStack()
    sb = lambda name, shape, dt: ex.enter_context(nc.sbuf_tensor(name, shape, dt))

    kT = sb("kT", [128, NCHUNK, T], BF16)
    h = sb("h", [128, NSLOT, D], F32)
    xbuf = sb("xbuf", [128, 10, D], F32)
    ztile = sb("ztile", [128, 1, D], F32)
    ktok = sb("ktok", [128, 2, D], BF16)
    z2tok = sb("z2tok", [128, 2, D], BF16)
    wTb = sb("wTb", [128, NCHUNK, D], BF16)
    uTb = sb("uTb", [128, NCHUNK, 128], BF16)
    gwTb = sb("gwTb", [128, NCHUNK, 8], BF16)
    vPb = sb("vPb", [128, D], BF16)

    ident = sb("identb", [128, 128], BF16)
    maskm = sb("maskmb", [128, 128], BF16)
    masko = sb("maskob", [128, 128], BF16)
    wm16 = sb("wm16b", [128, NSLOT], F32)
    emaf = sb("emaf", [128, NCHUNK], BF16)
    srow = sb("srow", [1, 8], F32)
    crow = sb("crow", [1, 8], F32)
    onesrow = sb("onesrow", [1, 128], F32)
    constcols = sb("constcols", [128, 8], F32)
    tiny = sb("tiny", [1, 8], F32)
    st6 = sb("st6", [128, 2, GS, 12], F32)
    mvg = sb("mvg", [128, 2, GS, 2], F32)
    rstd8 = sb("rstd8", [128, 2, GS], F32)
    rtmp8 = sb("rtmp8", [128, 2, GS], F32)
    ss8 = sb("ss8", [128, 2, GS], F32)
    inv8 = sb("inv8", [128, 2, GS], F32)
    mb8 = sb("mb8", [128, 2, GS], F32)
    colbuf = sb("colbuf", [128, 2, 16], F32)
    sraw16 = sb("sraw16", [128, NSLOT], F32)
    ecol16 = sb("ecol16", [128, NSLOT], F32)
    gtmp16 = sb("gtmp16", [128, NSLOT], F32)
    gate16 = sb("gate16", [128, NSLOT], F32)
    mv2 = sb("mv2", [128, NSLOT, 2], F32)
    st26 = sb("st26", [128, 2, 12], F32)
    rstd16 = sb("rstd16", [128, NSLOT], F32)
    rtmp16 = sb("rtmp16", [128, NSLOT], F32)
    nmax2 = sb("nmax2", [128, 2], F32)
    ssum2 = sb("ssum2", [128, 2], F32)
    rinv2 = sb("rinv2", [128, 2], F32)
    gexp2 = sb("gexp2", [128, 2, 8], F32)
    tgb = sb("tgb", [128, 2, 128], BF16)
    tgTb = sb("tgTb", [128, 2, 128], BF16)
    constf = sb("constf", [128, 2], F32)
    identf = sb("identf", [128, 128], F32)

    pst = [ex.enter_context(nc.psum_tensor(f"ps{i}", [128, 512], F32))
           for i in range(8)]
    pspools = {"big": [0, 1, 2, 3, 4], "tr": [6, 7], "one": [5]}
    psidx = {"big": 0, "tr": 0, "one": 0}

    def psalloc(pool="big"):
        lst = pspools[pool]
        i = lst[psidx[pool] % len(lst)]
        psidx[pool] += 1
        return pst[i], ("ps", i)

    sems = {s: ex.enter_context(nc.semaphore(s)) for s in ALLSEMS}

    wstage = xbuf
    emarow = ztile[0:1, 0, :]
    escr = xbuf[0:1, 0, :]
    embf = ktok[0:1, 0, :]
    gwbf = ktok[0:8, 1, :]

    p = P()

    p.op("gpsimd", lambda e: e.memset(constf[:, 0:1], EPS), writes=[("constf",)])
    p.op("gpsimd", lambda e: e.memset(constf[:, 1:2], 1e-24), writes=[("constf",)])
    p.op("scalar", lambda e: e.dma_start(maskm[:, :], mmd), writes=[("maskm",)], dma=True)
    p.op("scalar", lambda e: e.dma_start(masko[:, :], mod), writes=[("masko",)], dma=True)
    p.op("scalar", lambda e: e.dma_start(wm16[:, :], wmd), writes=[("wm16",)], dma=True)
    p.op("scalar", lambda e: e.dma_start(ident[:, :], idd), writes=[("ident",)], dma=True)
    p.op("vector", lambda e: e.tensor_copy(identf[:, :], ident[:, :]),
         reads=[("ident",)], writes=[("identf",)], cost=200.0)
    p.op("scalar", lambda e: e.dma_start(srow[:, 0:4], scald), writes=[("srow",)], dma=True)
    p.op("scalar", lambda e: e.dma_start(emarow, emad), writes=[("zt", 0)], dma=True)
    p.op("scalar", lambda e: e.dma_start(wstage[0:8, 1, :], gwd),
         writes=[("xbuf", 1)], dma=True)

    p.default_bonus = 60000.0
    p.op("scalar", lambda e: e.activation(crow[:, 4:5], srow[:, 0:1], AF.Exp),
         reads=[("srow",)], writes=[("crow",)])
    p.op("scalar", lambda e: e.activation(crow[:, 5:6], srow[:, 1:2], AF.Exp, scale=-1.0),
         reads=[("srow",)], writes=[("crow",)])
    p.op("scalar", lambda e: e.activation(crow[:, 6:7], srow[:, 2:3], AF.Exp, scale=-1.0),
         reads=[("srow",)], writes=[("crow",)])
    p.op("vector", lambda e: e.tensor_scalar(crow[:, 5:6], crow[:, 5:6], 1.0, None, ALU.add),
         reads=[("crow",)], writes=[("crow",)])
    p.op("vector", lambda e: e.reciprocal(crow[:, 2:3], crow[:, 5:6]),
         reads=[("crow",)], writes=[("crow",)])
    p.op("vector", lambda e: e.tensor_scalar(crow[:, 3:4], crow[:, 2:3], -1.0, 1.0, ALU.mult, ALU.add),
         reads=[("crow",)], writes=[("crow",)])
    p.op("vector", lambda e: e.tensor_scalar(crow[:, 6:7], crow[:, 6:7], 1.0, None, ALU.add),
         reads=[("crow",)], writes=[("crow",)])
    p.op("vector", lambda e: e.reciprocal(crow[:, 7:8], crow[:, 6:7]),
         reads=[("crow",)], writes=[("crow",)])
    p.op("vector", lambda e: e.tensor_tensor(crow[:, 0:1], crow[:, 4:5], crow[:, 7:8], ALU.mult),
         reads=[("crow",)], writes=[("crow",)])
    p.op("vector", lambda e: e.tensor_scalar(crow[:, 6:7], crow[:, 7:8], -1.0, 1.0, ALU.mult, ALU.add),
         reads=[("crow",)], writes=[("crow",)])
    p.op("vector", lambda e: e.tensor_tensor(crow[:, 1:2], crow[:, 4:5], crow[:, 6:7], ALU.mult),
         reads=[("crow",)], writes=[("crow",)])
    p.op("vector", lambda e: e.memset(onesrow[:, :], 1.0), writes=[("ones",)])
    psC, tC = psalloc("big")
    p.op("tensor", lambda e: e.matmul(psC[:, 0:8], onesrow[:, :], crow[:, :],
                                      start=True, stop=True),
         reads=[("ones",), ("crow",)], writes=[tC])
    p.op("vector", lambda e: e.tensor_copy(constcols[:, :], psC[:, 0:8]),
         reads=[tC], writes=[("constcols",)])
    A_col = constcols[:, 0:1]
    B_col = constcols[:, 1:2]
    al_col = constcols[:, 2:3]
    be_col = constcols[:, 3:4]

    p.op("scalar", lambda e: e.activation(escr, emarow, AF.Square,
                                          accum_out=tiny[:, 0:1]),
         reads=[("zt", 0)], writes=[("xbuf", 0), ("tiny",)])
    p.op("vector", lambda e: e.tensor_scalar(tiny[:, 0:1], tiny[:, 0:1], 1e-24, None, ALU.max),
         reads=[("tiny",)], writes=[("tiny",)])
    p.op("scalar", lambda e: e.activation(tiny[:, 1:2], tiny[:, 0:1], AF.Ln),
         reads=[("tiny",)], writes=[("tiny",)])
    p.op("scalar", lambda e: e.activation(tiny[:, 2:3], tiny[:, 1:2], AF.Exp, scale=-0.5),
         reads=[("tiny",)], writes=[("tiny",)])
    p.op("vector", lambda e: e.tensor_scalar(embf, emarow, tiny[:, 2:3], None, ALU.mult),
         reads=[("zt", 0), ("tiny",)], writes=[("ktok", 0)])
    psE, tE = psalloc("big")
    fns = []
    for c in range(NCHUNK):
        fns.append(lambda e, c=c: e.matmul(psE[:, c:c + 1], embf[:, c * 128:(c + 1) * 128],
                                           ident[0:1, 0:1], start=(c == 0), stop=(c == 7),
                                           skip_group_check=True))
    p.op("tensor", fns, reads=[("ktok", 0), ("ident",)], writes=[tE])
    p.op("vector", lambda e: e.tensor_copy(emaf[:, :], psE[:, 0:8]),
         reads=[tE], writes=[("emaf",)])

    p.op("vector", lambda e: e.tensor_copy(gwbf, wstage[0:8, 1, :]),
         reads=[("xbuf", 1)], writes=[("ktok", 1)])
    psG, tG = psalloc("big")
    fns = []
    for c in range(NCHUNK):
        fns.append(lambda e, c=c: e.matmul(psG[:, c * 8:(c + 1) * 8],
                                           gwbf[:, c * 128:(c + 1) * 128],
                                           ident[0:8, 0:8], start=(c == 0), stop=(c == 7),
                                           skip_group_check=True))
    p.op("tensor", fns, reads=[("ktok", 1), ("ident",)], writes=[tG])
    p.op("vector", lambda e: e.tensor_copy(
        gwTb[:, :, :], psG[:, 0:64].rearrange("p (c e) -> p c e", c=8)),
        reads=[tG], writes=[("gwTb",)])

    p.default_bonus = 0.0
    for c in range(NCHUNK):
        p.op("scalar", lambda e, c=c: e.dma_start(
            wstage[:, c % 2, :], wtd[c * 128:(c + 1) * 128, :]),
            writes=[("xbuf", c % 2)], dma=True, cost=2500.0)
        p.op("vector", lambda e, c=c: e.tensor_copy(wTb[:, c, :], wstage[:, c % 2, :]),
             reads=[("xbuf", c % 2)], writes=[("wTb",)], cost=600.0)
    p.op("scalar", lambda e: e.dma_start(
        wstage[:, 0, :].rearrange("p (c j) -> p c j", c=8),
        utd.rearrange("(c q) j -> q c j", q=128)),
        writes=[("xbuf", 0)], dma=True)
    p.op("vector", lambda e: e.tensor_copy(
        uTb[:, :, :], wstage[:, 0, :].rearrange("p (c j) -> p c j", c=8)),
        reads=[("xbuf", 0)], writes=[("uTb",)])
    p.op("scalar", lambda e: e.dma_start(wstage[:, 1, :], vpd),
         writes=[("xbuf", 1)], dma=True)
    p.op("vector", lambda e: e.tensor_scalar(vPb[:, :], wstage[:, 1, :], 0.0625, None, ALU.mult),
         reads=[("xbuf", 1)], writes=[("vPb",)])

    order = []
    for j in range(NSLOT):
        order.append(j)
        order.append(16 + j)

    def kcol(tid):
        return 2048 * (tid // 16) + 128 * (tid % 16)

    xview = xd.rearrange("(g t q) d -> q g t d", g=2, q=128)

    def emit_key_tile(gi, i, tid, ri):
        g2 = gi % 2
        if ri % 2 == 0:
            j = tid
            p.op("sync", lambda e, ri=ri, j=j: e.dma_start(
                xbuf[:, ri:ri + 2, :], xview[:, :, j, :]),
                writes=[("xbuf", ri), ("xbuf", ri + 1)], dma=True, cost=3700.0)
        p.op("vector", [
            lambda e: e.bn_stats(st6[:, g2, i, 0:6], xbuf[:, ri, 0:512]),
            lambda e: e.bn_stats(st6[:, g2, i, 6:12], xbuf[:, ri, 512:1024]),
        ], reads=[("xbuf", ri)], writes=[("st6", g2, i)], cost=1250.0)
        p.op("vector", lambda e: e.bn_aggr(mvg[:, g2, i, :], st6[:, g2, i, :]),
             reads=[("st6", g2, i)], writes=[("mv", g2, i)], cost=90.0)

    def emit_key_tile_b(gi, i, tid, ri):
        g2 = gi % 2
        buf = ri % 2
        mine = tid < 16
        gdst = h[:, tid, :] if mine else xbuf[:, ri, :]
        gtok = ("h", tid) if mine else ("xbuf", ri)
        p.op("scalar", lambda e: e.activation(
            gdst, xbuf[:, ri, :], AF.Gelu_apprx_tanh,
            bias=mb8[:, g2, i:i + 1], scale=rstd8[:, g2, i:i + 1]),
            reads=[("xbuf", ri), ("rstd8", g2), ("mb8", g2)], writes=[gtok],
            cost=1150.0, act_set="gelu")
        scr = xbuf[:, ri, :] if mine else ztile[:, 0, :]
        stok = ("xbuf", ri) if mine else ("zt", 0)
        p.op("scalar", lambda e: e.activation(
            scr, gdst, AF.Square, accum_out=ss8[:, g2, i:i + 1]),
            reads=[gtok], writes=[stok, ("ss", g2, i)], cost=1150.0)

    def emit_key_tile_c(gi, i, tid, ri):
        g2 = gi % 2
        buf = ri % 2
        mine = tid < 16
        gdst = h[:, tid, :] if mine else xbuf[:, ri, :]
        gtok = ("h", tid) if mine else ("xbuf", ri)
        col = kcol(tid)
        p.op("gpsimd", lambda e: e.tensor_tensor(
            ktok[:, buf, :], gdst,
            inv8[:, g2, i:i + 1].broadcast_to([128, D]), ALU.mult),
            reads=[gtok, ("inv8", g2)], writes=[("ktok", buf)], cost=2300.0)
        pa, ta = psalloc("tr")
        pb, tb = psalloc("tr")
        fns = []
        for c in range(NCHUNK):
            ps = pa if c < 4 else pb
            fns.append(lambda e, c=c, ps=ps: e.matmul(
                ps[:, (c % 4) * 128:(c % 4) * 128 + 128],
                ktok[:, buf, c * 128:(c + 1) * 128], ident[:, :],
                start=(c % 4 == 0), stop=(c % 4 == 3), skip_group_check=True))
        p.op("tensor", fns, reads=[("ktok", buf), ("ident",)], writes=[ta, tb],
             cost=900.0)
        p.op("scalar", lambda e: e.copy(
            kT[:, 0:4, col:col + 128],
            pa[:, :].rearrange("p (c f) -> p c f", c=4)),
            reads=[ta], writes=[("kT", tid, 0)], cost=580.0)
        p.op("scalar", lambda e: e.copy(
            kT[:, 4:8, col:col + 128],
            pb[:, :].rearrange("p (c f) -> p c f", c=4)),
            reads=[tb], writes=[("kT", tid, 1)], cost=580.0)

    def qk_slot(m):
        L = m + 1
        q, rem = divmod(L, 4)
        blocks = [[512 * i, 512, False] for i in range(q)]
        if rem:
            blocks.append([512 * q, 128 * rem, True])
        else:
            blocks[-1][2] = True
        allb = [("m", bs, W, iswin) for (bs, W, iswin) in blocks] + \
               [("o", bs, W, iswin) for (bs, W, iswin) in blocks]
        cbuf = m % 2
        cb_idx = 0
        groups = [allb[i:i + 2] for i in range(0, len(allb), 2)]
        qtok = [("kT", m, 0), ("kT", m, 1)]
        for gi2, grp in enumerate(groups):
            ptoks = []
            pts = []
            reads = list(qtok) + [("ident",)]
            for (side, bs, W, iswin) in grp:
                pt, tk = psalloc()
                pts.append(pt)
                ptoks.append(tk)
                base = 0 if side == "m" else 16
                for j in range(bs // 128, (bs + W - 1) // 128 + 1):
                    reads.append(("kT", base + j, 0))
                    reads.append(("kT", base + j, 1))
                if iswin:
                    reads.append(("maskm",) if side == "m" else ("masko",))
            has_ema = (gi2 == 0)
            if has_ema:
                pe_t, tk_e = psalloc("one")
                ptoks.append(tk_e)
                reads.append(("emaf",))
            fns = []
            for c in range(NCHUNK):
                for bi, (side, bs, W, iswin) in enumerate(grp):
                    off = (0 if side == "m" else 2048) + bs
                    fns.append(lambda e, c=c, pt=pts[bi], off=off, W=W, iswin=iswin: e.matmul(
                        pt[:, 0:W],
                        kT[:, c, 128 * m:128 * m + 128],
                        kT[:, c, off:off + W],
                        start=(c == 0), stop=(c == 7 and not iswin),
                        skip_group_check=True))
                if has_ema:
                    fns.append(lambda e, c=c, pe_t=pe_t: e.matmul(
                        pe_t[:, 0:1], kT[:, c, 128 * m:128 * m + 128],
                        emaf[:, c:c + 1],
                        start=(c == 0), stop=(c == 7), skip_group_check=True))
            for bi, (side, bs, W, iswin) in enumerate(grp):
                if iswin:
                    msk = maskm if side == "m" else masko
                    fns.append(lambda e, pt=pts[bi], W=W, msk=msk: e.matmul(
                        pt[:, W - 128:W], ident[:, :], msk[:, :],
                        start=False, stop=True, skip_group_check=True))
            mmcost = (8 * sum(W for (_, _, W, _) in grp)
                      + 128 * sum(1 for b in grp if b[3])) / 2.4 + 8 * 110.0
            p.op("tensor", fns, reads=reads, writes=ptoks, cost=mmcost)
            for bi, (side, bs, W, iswin) in enumerate(grp):
                p.op("vector", lambda e, pt=pts[bi], W=W, ci=cb_idx: e.reduce_max(
                    colbuf[:, cbuf, ci:ci + 1], pt[:, 0:W], axis=AX),
                    reads=[ptoks[bi]], writes=[("colbuf", cbuf)],
                    cost=(130 + W) / 0.96)
                cb_idx += 1
            if has_ema:
                p.op("vector", lambda e, pe_t=pe_t: e.tensor_copy(ecol16[:, m:m + 1], pe_t[:, 0:1]),
                     reads=[tk_e], writes=[("ecol", m)])
        p.op("vector", lambda e, n=cb_idx: e.reduce_max(
            sraw16[:, m:m + 1], colbuf[:, cbuf, 0:n], axis=AX),
            reads=[("colbuf", cbuf)], writes=[("sraw", m)])

    def keygroup(gi, t0, n):
        g2 = gi % 2
        tiles = order[t0: t0 + n]
        for i, tid in enumerate(tiles):
            emit_key_tile(gi, i, tid, (GS * gi + i) % 10)
        p.op("scalar", lambda e, g2=g2: e.activation(
            rtmp8[:, g2, :].unsqueeze(2), mvg[:, g2, :, 1:2], AF.Ln, bias=constf[:, 0:1]),
            reads=[("mv", g2, i) for i in range(GS)] + [("constf",)],
            writes=[("rtmp8", g2)], cost=310.0, act_set="lnexp")
        p.op("scalar", lambda e, g2=g2: e.activation(
            rstd8[:, g2, :], rtmp8[:, g2, :], AF.Exp, scale=-0.5),
            reads=[("rtmp8", g2)], writes=[("rstd8", g2)], cost=310.0, act_set="lnexp")
        p.op("vector", lambda e, g2=g2: e.scalar_tensor_tensor(
            mb8[:, g2, :].unsqueeze(2), mvg[:, g2, :, 0:1], -1.0,
            rstd8[:, g2, :].unsqueeze(2), ALU.mult, ALU.mult),
            reads=[("rstd8", g2)] + [("mv", g2, i) for i in range(GS)],
            writes=[("mb8", g2)], cost=160.0)
        for i, tid in enumerate(tiles):
            emit_key_tile_b(gi, i, tid, (GS * gi + i) % 10)
        p.op("vector", lambda e, g2=g2: e.tensor_scalar(
            ss8[:, g2, :], ss8[:, g2, :], 1e-24, None, ALU.max),
            reads=[("ss", g2, i) for i in range(GS)],
            writes=[("ss", g2, i) for i in range(GS)])
        p.op("scalar", lambda e, g2=g2: e.activation(
            rtmp8[:, g2, :], ss8[:, g2, :], AF.Ln),
            reads=[("ss", g2, i) for i in range(GS)], writes=[("rtmp8", g2)],
            cost=310.0, act_set="lnexp")
        p.op("scalar", lambda e, g2=g2: e.activation(
            inv8[:, g2, :], rtmp8[:, g2, :], AF.Exp, scale=-0.5),
            reads=[("rtmp8", g2)], writes=[("inv8", g2)], cost=310.0, act_set="lnexp")
        for j in range(n // 2):
            emit_key_tile_c(gi, 2 * j, tiles[2 * j], (t0 + 2 * j) % 10)
            emit_key_tile_c(gi, 2 * j + 1, tiles[2 * j + 1], (t0 + 2 * j + 1) % 10)
            qk_slot((t0 + 2 * j) // 2)


    def gates_batch(g4):
        lo, hi = 4 * g4, 4 * g4 + 4
        srd = [("sraw", m) for m in range(lo, hi)] + [("ecol", m) for m in range(lo, hi)]
        gt = gtmp16[:, lo:hi]
        ec = ecol16[:, lo:hi]
        if g4 == 0:
            p.op("vector", lambda e: e.tensor_tensor(gt, sraw16[:, lo:hi], ec, ALU.subtract),
                 reads=srd, writes=[("gtmp", g4)])
            p.op("vector", lambda e: e.tensor_tensor(gt, gt, wm16[:, lo:hi], ALU.mult),
                 reads=[("gtmp", g4), ("wm16",)], writes=[("gtmp", g4)])
            p.op("vector", lambda e: e.tensor_tensor(gt, gt, ec, ALU.add),
                 reads=[("gtmp", g4)] + [("ecol", m) for m in range(lo, hi)],
                 writes=[("gtmp", g4)])
            p.op("vector", lambda e: e.tensor_scalar(gt, gt, A_col, None, ALU.mult),
                 reads=[("gtmp", g4), ("constcols",)], writes=[("gtmp", g4)])
        else:
            p.op("vector", lambda e: e.tensor_scalar(gt, sraw16[:, lo:hi], A_col, None, ALU.mult),
                 reads=srd + [("constcols",)], writes=[("gtmp", g4)])
        p.op("vector", lambda e: e.scalar_tensor_tensor(
            gt, ec, B_col, gt, ALU.mult, ALU.add),
            reads=[("gtmp", g4), ("constcols",)] + [("ecol", m) for m in range(lo, hi)],
            writes=[("gtmp", g4)])
        p.op("scalar", lambda e: e.activation(gate16[:, lo:hi], gt, AF.Exp, scale=-1.0),
             reads=[("gtmp", g4)], writes=[("gates", g4)], cost=310.0, act_set="lnexp")
        p.op("vector", lambda e: e.tensor_scalar(gate16[:, lo:hi], gate16[:, lo:hi],
                                                 al_col, be_col, ALU.mult, ALU.add),
             reads=[("gates", g4), ("constcols",)], writes=[("gates", g4)])

    def g_slot(m):
        i = m % 10
        buf = m % 2
        if m % 2 == 0:
            p.op("scalar", lambda e, m=m, i=i: e.dma_start(
                xbuf[:, i:i + 2, :],
                xd[m * 128:(m + 2) * 128, :].rearrange("(t q) d -> q t d", q=128)),
                writes=[("xbuf", i), ("xbuf", i + 1)], dma=True, cost=3700.0)
        p.op("vector", lambda e, m=m, i=i: e.scalar_tensor_tensor(
            h[:, m, :], h[:, m, :], gate16[:, m:m + 1], xbuf[:, i, :],
            ALU.mult, ALU.add),
            reads=[("h", m), ("gates", m // 4), ("xbuf", i)], writes=[("h", m)],
            cost=1230.0)
        p.op("vector", [
            lambda e, m=m: e.bn_stats(st26[:, m % 2, 0:6], h[:, m, 0:512]),
            lambda e, m=m: e.bn_stats(st26[:, m % 2, 6:12], h[:, m, 512:1024]),
        ], reads=[("h", m)], writes=[("st26", m % 2)], cost=1250.0)
        p.op("vector", lambda e, m=m: e.bn_aggr(mv2[:, m, :], st26[:, m % 2, :]),
             reads=[("st26", m % 2)], writes=[("mv2", m)], cost=90.0)
        p.op("scalar", lambda e, m=m: e.activation(
            rtmp16[:, m:m + 1], mv2[:, m, 1:2], AF.Ln, bias=constf[:, 0:1]),
            reads=[("mv2", m), ("constf",)], writes=[("rtmp16", m)],
            cost=300.0, act_set="lnexp")
        p.op("scalar", lambda e, m=m: e.activation(
            rstd16[:, m:m + 1], rtmp16[:, m:m + 1], AF.Exp, scale=-0.5),
            reads=[("rtmp16", m)], writes=[("rstd16", m)], cost=300.0, act_set="lnexp")

    t0 = 0
    for gi, n in enumerate([GS] * (NTILE // GS)):
        keygroup(gi, t0, n)
        t0 += n

    for g4 in range(4):
        gates_batch(g4)
        for m in range(4 * g4, 4 * g4 + 4):
            g_slot(m)

    def ffn_part1(m):
        buf = m % 2
        p.op("vector", lambda e: e.scalar_tensor_tensor(
            rtmp16[:, m:m + 1], mv2[:, m, 0:1], -1.0, rstd16[:, m:m + 1],
            ALU.mult, ALU.mult),
            reads=[("mv2", m), ("rstd16", m)], writes=[("rtmp16", m)], cost=90.0)
        p.op("scalar", lambda e: e.activation(
            z2tok[:, buf, :], h[:, m, :], AF.Identity,
            bias=rtmp16[:, m:m + 1], scale=rstd16[:, m:m + 1]),
            reads=[("h", m), ("rtmp16", m), ("rstd16", m)], writes=[("z2", buf)],
            cost=700.0, act_set="lnexp")
        pa, ta = psalloc("tr")
        pb, tb = psalloc("tr")
        fns = []
        for c in range(NCHUNK):
            ps = pa if c < 4 else pb
            fns.append(lambda e, c=c, ps=ps: e.matmul(
                ps[:, (c % 4) * 128:(c % 4) * 128 + 128],
                z2tok[:, buf, c * 128:(c + 1) * 128], ident[:, :],
                start=(c % 4 == 0), stop=(c % 4 == 3), skip_group_check=True))
        p.op("tensor", fns, reads=[("z2", buf), ("ident",)], writes=[ta, tb],
             cost=900.0)
        col = 128 * m
        p.op("scalar", lambda e: e.copy(
            kT[:, 0:4, col:col + 128], pa[:, :].rearrange("p (c f) -> p c f", c=4)),
            reads=[ta], writes=[("kT", m, 0)], cost=580.0)
        p.op("scalar", lambda e: e.copy(
            kT[:, 4:8, col:col + 128], pb[:, :].rearrange("p (c f) -> p c f", c=4)),
            reads=[tb], writes=[("kT", m, 1)], cost=580.0)
        z2toks = [("kT", m, 0), ("kT", m, 1)]
        psA, tA = psalloc("big")
        psB, tB = psalloc("big")
        psT, tT = psalloc("one")
        fnsT = []
        fnsAB = []
        for c in range(NCHUNK):
            lhs = kT[:, c, 128 * m:128 * m + 128]
            fnsT.append(lambda e, c=c, lhs=lhs, psT=psT: e.matmul(
                psT[:, 0:128], lhs, uTb[:, c, :],
                start=(c == 0), stop=(c == 7), skip_group_check=True))
            fnsT.append(lambda e, c=c, lhs=lhs, psT=psT: e.matmul(
                psT[:, 128:136], lhs, gwTb[:, c, :],
                start=False, stop=(c == 7), skip_group_check=True))
            fnsAB.append(lambda e, c=c, lhs=lhs, psA=psA: e.matmul(
                psA[:, :], lhs, wTb[:, c, 0:512],
                start=(c == 0), stop=False, skip_group_check=True))
            fnsAB.append(lambda e, c=c, lhs=lhs, psB=psB: e.matmul(
                psB[:, :], lhs, wTb[:, c, 512:1024],
                start=(c == 0), stop=False, skip_group_check=True))
        p.op("tensor", fnsT, reads=z2toks + [("uTb",), ("gwTb",)],
             writes=[tT], cost=1350.0)
        p.op("tensor", fnsAB, reads=z2toks + [("wTb",)],
             writes=[tA, tB], cost=3550.0)
        p.op("vector", lambda e, psT=psT: e.reduce_max(
            nmax2[:, buf:buf + 1], psT[:, 128:136], axis=AX, negate=True),
            reads=[tT], writes=[("nmax", buf)])
        p.op("scalar", lambda e, psT=psT: e.activation(
            gexp2[:, buf, :], psT[:, 128:136], AF.Exp, bias=nmax2[:, buf:buf + 1]),
            reads=[tT, ("nmax", buf)], writes=[("gexp", buf)],
            cost=310.0, act_set="lnexp")
        p.op("vector", lambda e: e.reduce_sum(
            ssum2[:, buf:buf + 1], gexp2[:, buf, :], axis=AX),
            reads=[("gexp", buf)], writes=[("ssum", buf)])
        p.op("vector", lambda e: e.reciprocal(
            rinv2[:, buf:buf + 1], ssum2[:, buf:buf + 1]),
            reads=[("ssum", buf)], writes=[("rinv", buf)])
        p.op("vector", lambda e: e.tensor_scalar(
            gexp2[:, buf, :], gexp2[:, buf, :], rinv2[:, buf:buf + 1], None, ALU.mult),
            reads=[("gexp", buf), ("rinv", buf)], writes=[("gexp", buf)])
        p.op("vector", lambda e, psT=psT: e.tensor_tensor(
            tgb[:, buf, :].rearrange("p (ee r) -> p ee r", ee=8),
            psT[:, 0:128].rearrange("p (ee r) -> p ee r", ee=8),
            gexp2[:, buf, :].unsqueeze(2).broadcast_to([128, 8, 16]),
            ALU.mult),
            reads=[tT, ("gexp", buf)], writes=[("tg", buf)], cost=280.0)
        return (m, buf, psA, tA, psB, tB, psT, tT)

    def ffn_part2(st):
        m, buf, psA, tA, psB, tB, psT, tT = st
        psT2, tT2 = psalloc("tr")
        p.op("tensor", lambda e: e.matmul(
            psT2[:, 0:128], tgb[:, buf, :], ident[:, :],
            start=True, stop=True, skip_group_check=True),
            reads=[("tg", buf), ("ident",)], writes=[tT2])
        p.op("scalar", lambda e: e.copy(tgTb[:, buf, :], psT2[:, 0:128]),
             reads=[tT2], writes=[("tgT", buf)], cost=280.0)
        p.op("tensor", [
            lambda e: e.matmul(psA[:, :], tgTb[:, buf, :], vPb[:, 0:512],
                               start=False, stop=True, skip_group_check=True),
            lambda e: e.matmul(psB[:, :], tgTb[:, buf, :], vPb[:, 512:1024],
                               start=False, stop=True, skip_group_check=True),
        ], reads=[("tgT", buf), ("vPb",)], writes=[tA, tB], cost=560.0)
        p.op("vector", lambda e: e.scalar_tensor_tensor(
            h[:, m, 0:512], psA[:, :], 1.0, h[:, m, 0:512], ALU.bypass, ALU.add),
            reads=[tA, ("h", m)], writes=[("h", m)], cost=660.0)
        p.op("vector", lambda e: e.scalar_tensor_tensor(
            h[:, m, 512:1024], psB[:, :], 1.0, h[:, m, 512:1024], ALU.bypass, ALU.add),
            reads=[tB, ("h", m)], writes=[("h", m)], cost=660.0)
        if m % 2 == 1:
            p.op("sync", lambda e: e.dma_start(
                yd[(m - 1) * 128:(m + 1) * 128, :].rearrange("(t q) d -> q t d", q=128),
                h[:, m - 1:m + 1, :]),
                reads=[("h", m - 1), ("h", m)], dma=True, cost=3700.0,
                prio_bonus=40000.0)

    prev = None
    for m in range(NSLOT):
        st = ffn_part1(m)
        if prev is not None:
            ffn_part2(prev)
        prev = st
    ffn_part2(prev)

    est = p.emit(nc, sems)
    ex.close()
    return nc, est


def kernel(L0_attn_x, L0_attn_ema_out, L0_attn_logit_decay, L0_attn_log_tau,
           L0_attn_log_blend, L0_attn_logit_blend_seq, L0_ffn_weight,
           L0_ffn_gate_w, L0_ffn_lora_U, L0_ffn_lora_V, **_unused):
    x = np.ascontiguousarray(np.asarray(L0_attn_x, dtype=np.float32))
    W = np.asarray(L0_ffn_weight, dtype=np.float32)
    gw = np.ascontiguousarray(np.asarray(L0_ffn_gate_w, dtype=np.float32))
    U = np.asarray(L0_ffn_lora_U, dtype=np.float32)
    V = np.asarray(L0_ffn_lora_V, dtype=np.float32)
    ema = np.ascontiguousarray(np.asarray(L0_attn_ema_out, dtype=np.float32).reshape(1, D))
    scal = np.array([[np.float32(L0_attn_log_tau), np.float32(L0_attn_log_blend),
                      np.float32(L0_attn_logit_blend_seq),
                      np.float32(L0_attn_logit_decay)]], dtype=np.float32)
    wT = np.ascontiguousarray(W.T)
    uT = np.ascontiguousarray(U.reshape(128, D).T)
    vP = np.ascontiguousarray(V.transpose(0, 2, 1).reshape(128, D))
    ident = np.eye(128, dtype=ml_dtypes.bfloat16)
    ii, jj = np.indices((128, 128))
    maskM = np.where(jj >= ii, MASKVAL, 0.0).astype(ml_dtypes.bfloat16)
    maskO1 = np.where(jj >= ii + 1, MASKVAL, 0.0).astype(ml_dtypes.bfloat16)
    wm0 = np.ones((128, NSLOT), np.float32)
    wm0[0, 0] = 0.0
    wm1 = np.ones((128, NSLOT), np.float32)

    nc, _est = build_program()
    in_maps = []
    for c in range(8):
        b, rho = c // 2, c % 2
        xg = np.ascontiguousarray(
            np.concatenate([x[b, rho::2], x[b, 1 - rho::2]], axis=0))
        in_maps.append({
            "x": xg, "wt": wT, "ut": uT, "vp": vP, "gw": gw, "ema": ema,
            "scal": scal, "maskm": maskM,
            "masko": maskM if rho == 0 else maskO1,
            "wm16": wm0 if rho == 0 else wm1,
            "ident": ident,
        })
    res = run_bass_kernel_spmd(nc, in_maps, core_ids=list(range(8)))
    out = np.empty((4, T, D), dtype=np.float32)
    for c in range(8):
        b, rho = c // 2, c % 2
        out[b, rho::2, :] = res.results[c]["y"]
    return out


if __name__ == "__main__":
    nc, est = build_program()
    n = sum(1 for _ in nc.all_instructions())
    print("built; instructions:", n, " scheduler-estimated span:", est / 1e3, "us")


# version 45
# speedup vs baseline: 1.0587x; 1.0587x over previous
import contextlib

import numpy as np
import ml_dtypes

import concourse.bass as bass
import concourse.mybir as mybir
from concourse.bass_utils import run_bass_kernel_spmd

F32 = mybir.dt.float32
BF16 = mybir.dt.bfloat16
AF = mybir.ActivationFunctionType
ALU = mybir.AluOpType
AX = mybir.AxisListType.X

D = 1024
T = 4096
NSLOT = 16
NCHUNK = 8
NTILE = 32
GS = 4
EPS = 1e-5
MASKVAL = -30000.0

ENGINES = ("sync", "tensor", "vector", "scalar", "gpsimd")
ESEM = {"tensor": "pe", "vector": "dve", "scalar": "act", "gpsimd": "pool",
        "sync": "spe"}
QPOOL = {"sync": [f"qs{i}" for i in range(8)],
         "scalar": [f"qc{i}" for i in range(4)]}
ALLSEMS = ["pe", "dve", "act", "pool", "spe"] + QPOOL["sync"] + QPOOL["scalar"]


class Op:
    __slots__ = ("oid", "engine", "fns", "dma", "cost", "act_set", "preds",
                 "succs", "prio", "start", "finish", "semn", "semv", "bonus")

    def __init__(self, oid, engine, fns, dma, cost, act_set):
        self.oid = oid
        self.engine = engine
        self.fns = fns
        self.dma = dma
        self.cost = cost
        self.act_set = act_set
        self.preds = set()
        self.succs = []
        self.prio = 0.0
        self.start = 0.0
        self.finish = 0.0
        self.semn = None
        self.semv = 0
        self.bonus = 0.0


ACT_SWITCH_NS = 1400.0


class P:

    def __init__(self):
        self.ops = []
        self.lastw = {}
        self.readers = {}
        self.default_bonus = 0.0

    def op(self, engine, fns, reads=(), writes=(), dma=False, cost=150.0,
           act_set=None, prio_bonus=0.0):
        if callable(fns):
            fns = [fns]
        o = Op(len(self.ops), engine, fns, dma, float(cost), act_set)
        o.bonus = float(prio_bonus) if prio_bonus else self.default_bonus
        for t in reads:
            w = self.lastw.get(t)
            if w is not None:
                o.preds.add(w)
        for t in writes:
            w = self.lastw.get(t)
            if w is not None:
                o.preds.add(w)
            for r in self.readers.get(t, ()):
                o.preds.add(r)
        o.preds.discard(o.oid)
        for t in writes:
            self.lastw[t] = o.oid
            self.readers[t] = []
        for t in reads:
            self.readers.setdefault(t, []).append(o.oid)
        self.ops.append(o)
        return o.oid

    def schedule(self):
        import heapq
        ops = self.ops
        for o in ops:
            for pid in o.preds:
                ops[pid].succs.append(o.oid)

        def blevel(scale):
            prio = [0.0] * len(ops)
            for o in reversed(ops):
                m = 0.0
                for s in o.succs:
                    if prio[s] > m:
                        m = prio[s]
                prio[o.oid] = o.cost * scale.get(o.engine, 1.0) + o.bonus + m
            return prio

        def greedy(prio, simscale):
            indeg = [len(o.preds) for o in ops]
            ready = {e: [] for e in ENGINES}
            data_ready = [0.0] * len(ops)
            finish = [0.0] * len(ops)
            for o in ops:
                if indeg[o.oid] == 0:
                    heapq.heappush(ready[o.engine], (-prio[o.oid], o.oid))
            engine_free = {e: 0.0 for e in ENGINES}
            queue_free = {"sync": 0.0, "scalar": 0.0}
            act_cur = None
            sched = {e: [] for e in ENGINES}
            n_left = len(ops)
            while n_left:
                best = None
                for e in ENGINES:
                    if not ready[e]:
                        continue
                    nprio, oid = ready[e][0]
                    o = ops[oid]
                    st = max(engine_free[e], data_ready[oid])
                    if o.dma:
                        st = max(st, queue_free[e])
                    if e == "scalar" and o.act_set and act_cur and \
                            o.act_set != act_cur:
                        st += ACT_SWITCH_NS
                    if best is None or st < best[0] or \
                            (st == best[0] and nprio < best[1]):
                        best = (st, nprio, e, oid)
                st, _, e, oid = best
                heapq.heappop(ready[e])
                o = ops[oid]
                finish[oid] = st + o.cost * simscale.get(e, 1.0)
                engine_free[e] = finish[oid] if not o.dma else st + 100.0
                if o.dma:
                    queue_free[e] = finish[oid]
                if e == "scalar" and o.act_set:
                    act_cur = o.act_set
                sched[e].append(oid)
                n_left -= 1
                for s in o.succs:
                    if finish[oid] > data_ready[s]:
                        data_ready[s] = finish[oid]
                    indeg[s] -= 1
                    if indeg[s] == 0:
                        heapq.heappush(ready[ops[s].engine], (-prio[s], s))
            return max(finish), sched

        est, sched = greedy(blevel({"vector": 2.5, "scalar": 1.35}),
                            {"vector": 1.25})
        self.sched = sched
        return est

    def emit(self, nc, sems):
        est = self.schedule()
        ops = self.ops
        cnt = {s: 0 for s in ALLSEMS}
        qidx = {"sync": 0, "scalar": 0}
        for e in ENGINES:
            for oid in self.sched[e]:
                o = ops[oid]
                if o.dma:
                    pool = QPOOL[e]
                    o.semn = pool[qidx[e] % len(pool)]
                    qidx[e] += 1
                    cnt[o.semn] += 16
                else:
                    o.semn = ESEM[e]
                    cnt[o.semn] += 1
                o.semv = cnt[o.semn]
        totals = dict(cnt)
        with nc.Block() as block:
            for ename in ENGINES:
                def section(eng, ename=ename):
                    clk = {}
                    for oid in self.sched[ename]:
                        o = ops[oid]
                        waits = {}
                        for pid in o.preds:
                            p_ = ops[pid]
                            if clk.get(p_.semn, 0) < p_.semv:
                                if waits.get(p_.semn, 0) < p_.semv:
                                    waits[p_.semn] = p_.semv
                        if o.dma and o.semv > 16:
                            prev = o.semv - 16
                            if clk.get(o.semn, 0) < prev and \
                                    waits.get(o.semn, 0) < prev:
                                waits[o.semn] = prev
                        for s, v in sorted(waits.items()):
                            eng.wait_ge(sems[s], v)
                            if clk.get(s, 0) < v:
                                clk[s] = v
                        inst = None
                        for fn in o.fns:
                            inst = fn(eng)
                        if inst is not None:
                            inst.then_inc(sems[o.semn], 16 if o.dma else 1)
                    if ename == "gpsimd":
                        for s in QPOOL["sync"] + QPOOL["scalar"]:
                            if totals.get(s, 0) > 0:
                                eng.wait_ge(sems[s], totals[s])
                getattr(block, ename)(section)
        return est


def build_program():
    nc = bass.Bass("TRN2", target_bir_lowering=False)

    xd = nc.dram_tensor("x", [T, D], F32, kind="ExternalInput").ap()
    wtd = nc.dram_tensor("wt", [D, D], F32, kind="ExternalInput").ap()
    utd = nc.dram_tensor("ut", [D, 128], F32, kind="ExternalInput").ap()
    vpd = nc.dram_tensor("vp", [128, D], F32, kind="ExternalInput").ap()
    gwd = nc.dram_tensor("gw", [8, D], F32, kind="ExternalInput").ap()
    emad = nc.dram_tensor("ema", [1, D], F32, kind="ExternalInput").ap()
    scald = nc.dram_tensor("scal", [1, 4], F32, kind="ExternalInput").ap()
    mmd = nc.dram_tensor("maskm", [128, 128], BF16, kind="ExternalInput").ap()
    mod = nc.dram_tensor("masko", [128, 128], BF16, kind="ExternalInput").ap()
    wmd = nc.dram_tensor("wm16", [128, NSLOT], F32, kind="ExternalInput").ap()
    idd = nc.dram_tensor("ident", [128, 128], BF16, kind="ExternalInput").ap()
    yd = nc.dram_tensor("y", [NSLOT * 128, D], F32, kind="ExternalOutput").ap()

    ex = contextlib.ExitStack()
    sb = lambda name, shape, dt: ex.enter_context(nc.sbuf_tensor(name, shape, dt))

    kT = sb("kT", [128, NCHUNK, T], BF16)
    h = sb("h", [128, NSLOT, D], F32)
    xbuf = sb("xbuf", [128, 10, D], F32)
    ztile = sb("ztile", [128, 1, D], F32)
    ktok = sb("ktok", [128, 2, D], BF16)
    z2tok = sb("z2tok", [128, 2, D], BF16)
    wTb = sb("wTb", [128, NCHUNK, D], BF16)
    uTb = sb("uTb", [128, NCHUNK, 128], BF16)
    gwTb = sb("gwTb", [128, NCHUNK, 8], BF16)
    vPb = sb("vPb", [128, D], BF16)

    ident = sb("identb", [128, 128], BF16)
    maskm = sb("maskmb", [128, 128], BF16)
    masko = sb("maskob", [128, 128], BF16)
    wm16 = sb("wm16b", [128, NSLOT], F32)
    emaf = sb("emaf", [128, NCHUNK], BF16)
    srow = sb("srow", [1, 8], F32)
    crow = sb("crow", [1, 8], F32)
    onesrow = sb("onesrow", [1, 128], F32)
    constcols = sb("constcols", [128, 8], F32)
    tiny = sb("tiny", [1, 8], F32)
    st6 = sb("st6", [128, 2, GS, 12], F32)
    mvg = sb("mvg", [128, 2, GS, 2], F32)
    rstd8 = sb("rstd8", [128, 2, GS], F32)
    rtmp8 = sb("rtmp8", [128, 2, GS], F32)
    ss8 = sb("ss8", [128, 2, GS], F32)
    inv8 = sb("inv8", [128, 2, GS], F32)
    mb8 = sb("mb8", [128, 2, GS], F32)
    colbuf = sb("colbuf", [128, 2, 16], F32)
    sraw16 = sb("sraw16", [128, NSLOT], F32)
    ecol16 = sb("ecol16", [128, NSLOT], F32)
    gtmp16 = sb("gtmp16", [128, NSLOT], F32)
    gate16 = sb("gate16", [128, NSLOT], F32)
    mv2 = sb("mv2", [128, NSLOT, 2], F32)
    st26 = sb("st26", [128, 2, 12], F32)
    rstd16 = sb("rstd16", [128, NSLOT], F32)
    rtmp16 = sb("rtmp16", [128, NSLOT], F32)
    nmax2 = sb("nmax2", [128, 2], F32)
    ssum2 = sb("ssum2", [128, 2], F32)
    rinv2 = sb("rinv2", [128, 2], F32)
    gexp2 = sb("gexp2", [128, 2, 8], F32)
    tgb = sb("tgb", [128, 2, 128], BF16)
    tgTb = sb("tgTb", [128, 2, 128], BF16)
    constf = sb("constf", [128, 2], F32)
    identf = sb("identf", [128, 128], F32)

    pst = [ex.enter_context(nc.psum_tensor(f"ps{i}", [128, 512], F32))
           for i in range(8)]
    pspools = {"big": [0, 1, 2, 3, 4], "tr": [6, 7], "one": [5]}
    psidx = {"big": 0, "tr": 0, "one": 0}

    def psalloc(pool="big"):
        lst = pspools[pool]
        i = lst[psidx[pool] % len(lst)]
        psidx[pool] += 1
        return pst[i], ("ps", i)

    sems = {s: ex.enter_context(nc.semaphore(s)) for s in ALLSEMS}

    wstage = xbuf
    emarow = ztile[0:1, 0, :]
    escr = xbuf[0:1, 0, :]
    embf = ktok[0:1, 0, :]
    gwbf = ktok[0:8, 1, :]

    p = P()

    p.op("gpsimd", lambda e: e.memset(constf[:, 0:1], EPS), writes=[("constf",)])
    p.op("gpsimd", lambda e: e.memset(constf[:, 1:2], 1e-24), writes=[("constf",)])
    p.op("scalar", lambda e: e.dma_start(maskm[:, :], mmd), writes=[("maskm",)], dma=True)
    p.op("scalar", lambda e: e.dma_start(masko[:, :], mod), writes=[("masko",)], dma=True)
    p.op("scalar", lambda e: e.dma_start(wm16[:, :], wmd), writes=[("wm16",)], dma=True)
    p.op("scalar", lambda e: e.dma_start(ident[:, :], idd), writes=[("ident",)], dma=True)
    p.op("vector", lambda e: e.tensor_copy(identf[:, :], ident[:, :]),
         reads=[("ident",)], writes=[("identf",)], cost=200.0)
    p.op("scalar", lambda e: e.dma_start(srow[:, 0:4], scald), writes=[("srow",)], dma=True)
    p.op("scalar", lambda e: e.dma_start(emarow, emad), writes=[("zt", 0)], dma=True)
    p.op("scalar", lambda e: e.dma_start(wstage[0:8, 1, :], gwd),
         writes=[("xbuf", 1)], dma=True)

    p.default_bonus = 60000.0
    p.op("scalar", lambda e: e.activation(crow[:, 4:5], srow[:, 0:1], AF.Exp),
         reads=[("srow",)], writes=[("crow",)])
    p.op("scalar", lambda e: e.activation(crow[:, 5:6], srow[:, 1:2], AF.Exp, scale=-1.0),
         reads=[("srow",)], writes=[("crow",)])
    p.op("scalar", lambda e: e.activation(crow[:, 6:7], srow[:, 2:3], AF.Exp, scale=-1.0),
         reads=[("srow",)], writes=[("crow",)])
    p.op("vector", lambda e: e.tensor_scalar(crow[:, 5:6], crow[:, 5:6], 1.0, None, ALU.add),
         reads=[("crow",)], writes=[("crow",)])
    p.op("vector", lambda e: e.reciprocal(crow[:, 2:3], crow[:, 5:6]),
         reads=[("crow",)], writes=[("crow",)])
    p.op("vector", lambda e: e.tensor_scalar(crow[:, 3:4], crow[:, 2:3], -1.0, 1.0, ALU.mult, ALU.add),
         reads=[("crow",)], writes=[("crow",)])
    p.op("vector", lambda e: e.tensor_scalar(crow[:, 6:7], crow[:, 6:7], 1.0, None, ALU.add),
         reads=[("crow",)], writes=[("crow",)])
    p.op("vector", lambda e: e.reciprocal(crow[:, 7:8], crow[:, 6:7]),
         reads=[("crow",)], writes=[("crow",)])
    p.op("vector", lambda e: e.tensor_tensor(crow[:, 0:1], crow[:, 4:5], crow[:, 7:8], ALU.mult),
         reads=[("crow",)], writes=[("crow",)])
    p.op("vector", lambda e: e.tensor_scalar(crow[:, 6:7], crow[:, 7:8], -1.0, 1.0, ALU.mult, ALU.add),
         reads=[("crow",)], writes=[("crow",)])
    p.op("vector", lambda e: e.tensor_tensor(crow[:, 1:2], crow[:, 4:5], crow[:, 6:7], ALU.mult),
         reads=[("crow",)], writes=[("crow",)])
    p.op("vector", lambda e: e.memset(onesrow[:, :], 1.0), writes=[("ones",)])
    psC, tC = psalloc("big")
    p.op("tensor", lambda e: e.matmul(psC[:, 0:8], onesrow[:, :], crow[:, :],
                                      start=True, stop=True),
         reads=[("ones",), ("crow",)], writes=[tC])
    p.op("vector", lambda e: e.tensor_copy(constcols[:, :], psC[:, 0:8]),
         reads=[tC], writes=[("constcols",)])
    A_col = constcols[:, 0:1]
    B_col = constcols[:, 1:2]
    al_col = constcols[:, 2:3]
    be_col = constcols[:, 3:4]

    p.op("scalar", lambda e: e.activation(escr, emarow, AF.Square,
                                          accum_out=tiny[:, 0:1]),
         reads=[("zt", 0)], writes=[("xbuf", 0), ("tiny",)])
    p.op("vector", lambda e: e.tensor_scalar(tiny[:, 0:1], tiny[:, 0:1], 1e-24, None, ALU.max),
         reads=[("tiny",)], writes=[("tiny",)])
    p.op("scalar", lambda e: e.activation(tiny[:, 1:2], tiny[:, 0:1], AF.Ln),
         reads=[("tiny",)], writes=[("tiny",)])
    p.op("scalar", lambda e: e.activation(tiny[:, 2:3], tiny[:, 1:2], AF.Exp, scale=-0.5),
         reads=[("tiny",)], writes=[("tiny",)])
    p.op("vector", lambda e: e.tensor_scalar(embf, emarow, tiny[:, 2:3], None, ALU.mult),
         reads=[("zt", 0), ("tiny",)], writes=[("ktok", 0)])
    psE, tE = psalloc("big")
    fns = []
    for c in range(NCHUNK):
        fns.append(lambda e, c=c: e.matmul(psE[:, c:c + 1], embf[:, c * 128:(c + 1) * 128],
                                           ident[0:1, 0:1], start=(c == 0), stop=(c == 7),
                                           skip_group_check=True))
    p.op("tensor", fns, reads=[("ktok", 0), ("ident",)], writes=[tE])
    p.op("vector", lambda e: e.tensor_copy(emaf[:, :], psE[:, 0:8]),
         reads=[tE], writes=[("emaf",)])

    p.op("vector", lambda e: e.tensor_copy(gwbf, wstage[0:8, 1, :]),
         reads=[("xbuf", 1)], writes=[("ktok", 1)])
    psG, tG = psalloc("big")
    fns = []
    for c in range(NCHUNK):
        fns.append(lambda e, c=c: e.matmul(psG[:, c * 8:(c + 1) * 8],
                                           gwbf[:, c * 128:(c + 1) * 128],
                                           ident[0:8, 0:8], start=(c == 0), stop=(c == 7),
                                           skip_group_check=True))
    p.op("tensor", fns, reads=[("ktok", 1), ("ident",)], writes=[tG])
    p.op("vector", lambda e: e.tensor_copy(
        gwTb[:, :, :], psG[:, 0:64].rearrange("p (c e) -> p c e", c=8)),
        reads=[tG], writes=[("gwTb",)])

    p.default_bonus = 0.0
    for c in range(NCHUNK):
        p.op("scalar", lambda e, c=c: e.dma_start(
            wstage[:, c % 2, :], wtd[c * 128:(c + 1) * 128, :]),
            writes=[("xbuf", c % 2)], dma=True, cost=2500.0)
        p.op("vector", lambda e, c=c: e.tensor_copy(wTb[:, c, :], wstage[:, c % 2, :]),
             reads=[("xbuf", c % 2)], writes=[("wTb",)], cost=600.0)
    p.op("scalar", lambda e: e.dma_start(
        wstage[:, 0, :].rearrange("p (c j) -> p c j", c=8),
        utd.rearrange("(c q) j -> q c j", q=128)),
        writes=[("xbuf", 0)], dma=True)
    p.op("vector", lambda e: e.tensor_copy(
        uTb[:, :, :], wstage[:, 0, :].rearrange("p (c j) -> p c j", c=8)),
        reads=[("xbuf", 0)], writes=[("uTb",)])
    p.op("scalar", lambda e: e.dma_start(wstage[:, 1, :], vpd),
         writes=[("xbuf", 1)], dma=True)
    p.op("vector", lambda e: e.tensor_scalar(vPb[:, :], wstage[:, 1, :], 0.0625, None, ALU.mult),
         reads=[("xbuf", 1)], writes=[("vPb",)])

    order = []
    for j in range(NSLOT):
        order.append(j)
        order.append(16 + j)

    def kcol(tid):
        return 2048 * (tid // 16) + 128 * (tid % 16)

    xview = xd.rearrange("(g t q) d -> q g t d", g=2, q=128)

    def emit_key_tile(gi, i, tid, ri):
        g2 = gi % 2
        if ri % 2 == 0:
            j = tid
            p.op("sync", lambda e, ri=ri, j=j: e.dma_start(
                xbuf[:, ri:ri + 2, :], xview[:, :, j, :]),
                writes=[("xbuf", ri), ("xbuf", ri + 1)], dma=True, cost=3700.0)
        p.op("vector", [
            lambda e: e.bn_stats(st6[:, g2, i, 0:6], xbuf[:, ri, 0:512]),
            lambda e: e.bn_stats(st6[:, g2, i, 6:12], xbuf[:, ri, 512:1024]),
        ], reads=[("xbuf", ri)], writes=[("st6", g2, i)], cost=1250.0)
        p.op("vector", lambda e: e.bn_aggr(mvg[:, g2, i, :], st6[:, g2, i, :]),
             reads=[("st6", g2, i)], writes=[("mv", g2, i)], cost=90.0)

    def emit_key_tile_b(gi, i, tid, ri):
        g2 = gi % 2
        buf = ri % 2
        mine = tid < 16
        gdst = h[:, tid, :] if mine else xbuf[:, ri, :]
        gtok = ("h", tid) if mine else ("xbuf", ri)
        p.op("scalar", lambda e: e.activation(
            gdst, xbuf[:, ri, :], AF.Gelu_apprx_tanh,
            bias=mb8[:, g2, i:i + 1], scale=rstd8[:, g2, i:i + 1]),
            reads=[("xbuf", ri), ("rstd8", g2), ("mb8", g2)], writes=[gtok],
            cost=1150.0, act_set="gelu")
        scr = xbuf[:, ri, :] if mine else ztile[:, 0, :]
        stok = ("xbuf", ri) if mine else ("zt", 0)
        if tid % 2 == 0:
            p.op("scalar", lambda e: e.activation(
                scr, gdst, AF.Square, accum_out=ss8[:, g2, i:i + 1]),
                reads=[gtok], writes=[stok, ("ss", g2, i)], cost=1150.0)
        else:
            p.op("vector", lambda e: e.scalar_tensor_tensor(
                scr, gdst, 1.0, gdst, ALU.bypass, ALU.mult,
                accum_out=ss8[:, g2, i:i + 1]),
                reads=[gtok], writes=[stok, ("ss", g2, i)], cost=1230.0)

    def emit_key_tile_c(gi, i, tid, ri):
        g2 = gi % 2
        buf = ri % 2
        mine = tid < 16
        gdst = h[:, tid, :] if mine else xbuf[:, ri, :]
        gtok = ("h", tid) if mine else ("xbuf", ri)
        col = kcol(tid)
        p.op("gpsimd", lambda e: e.tensor_tensor(
            ktok[:, buf, :], gdst,
            inv8[:, g2, i:i + 1].broadcast_to([128, D]), ALU.mult),
            reads=[gtok, ("inv8", g2)], writes=[("ktok", buf)], cost=2300.0)
        pa, ta = psalloc("tr")
        pb, tb = psalloc("tr")
        fns = []
        for c in range(NCHUNK):
            ps = pa if c < 4 else pb
            fns.append(lambda e, c=c, ps=ps: e.matmul(
                ps[:, (c % 4) * 128:(c % 4) * 128 + 128],
                ktok[:, buf, c * 128:(c + 1) * 128], ident[:, :],
                start=(c % 4 == 0), stop=(c % 4 == 3), skip_group_check=True))
        p.op("tensor", fns, reads=[("ktok", buf), ("ident",)], writes=[ta, tb],
             cost=900.0)
        p.op("vector", lambda e: e.tensor_copy(
            kT[:, 0:4, col:col + 128],
            pa[:, :].rearrange("p (c f) -> p c f", c=4)),
            reads=[ta], writes=[("kT", tid, 0)], cost=660.0)
        p.op("scalar", lambda e: e.copy(
            kT[:, 4:8, col:col + 128],
            pb[:, :].rearrange("p (c f) -> p c f", c=4)),
            reads=[tb], writes=[("kT", tid, 1)], cost=580.0)

    def qk_slot(m):
        L = m + 1
        q, rem = divmod(L, 4)
        blocks = [[512 * i, 512, False] for i in range(q)]
        if rem:
            blocks.append([512 * q, 128 * rem, True])
        else:
            blocks[-1][2] = True
        allb = [("m", bs, W, iswin) for (bs, W, iswin) in blocks] + \
               [("o", bs, W, iswin) for (bs, W, iswin) in blocks]
        cbuf = m % 2
        cb_idx = 0
        groups = [allb[i:i + 2] for i in range(0, len(allb), 2)]
        qtok = [("kT", m, 0), ("kT", m, 1)]
        for gi2, grp in enumerate(groups):
            ptoks = []
            pts = []
            reads = list(qtok) + [("ident",)]
            for (side, bs, W, iswin) in grp:
                pt, tk = psalloc()
                pts.append(pt)
                ptoks.append(tk)
                base = 0 if side == "m" else 16
                for j in range(bs // 128, (bs + W - 1) // 128 + 1):
                    reads.append(("kT", base + j, 0))
                    reads.append(("kT", base + j, 1))
                if iswin:
                    reads.append(("maskm",) if side == "m" else ("masko",))
            has_ema = (gi2 == 0)
            if has_ema:
                pe_t, tk_e = psalloc("one")
                ptoks.append(tk_e)
                reads.append(("emaf",))
            fns = []
            for c in range(NCHUNK):
                for bi, (side, bs, W, iswin) in enumerate(grp):
                    off = (0 if side == "m" else 2048) + bs
                    fns.append(lambda e, c=c, pt=pts[bi], off=off, W=W, iswin=iswin: e.matmul(
                        pt[:, 0:W],
                        kT[:, c, 128 * m:128 * m + 128],
                        kT[:, c, off:off + W],
                        start=(c == 0), stop=(c == 7 and not iswin),
                        skip_group_check=True))
                if has_ema:
                    fns.append(lambda e, c=c, pe_t=pe_t: e.matmul(
                        pe_t[:, 0:1], kT[:, c, 128 * m:128 * m + 128],
                        emaf[:, c:c + 1],
                        start=(c == 0), stop=(c == 7), skip_group_check=True))
            for bi, (side, bs, W, iswin) in enumerate(grp):
                if iswin:
                    msk = maskm if side == "m" else masko
                    fns.append(lambda e, pt=pts[bi], W=W, msk=msk: e.matmul(
                        pt[:, W - 128:W], ident[:, :], msk[:, :],
                        start=False, stop=True, skip_group_check=True))
            mmcost = (8 * sum(W for (_, _, W, _) in grp)
                      + 128 * sum(1 for b in grp if b[3])) / 2.4 + 8 * 110.0
            p.op("tensor", fns, reads=reads, writes=ptoks, cost=mmcost)
            for bi, (side, bs, W, iswin) in enumerate(grp):
                p.op("vector", lambda e, pt=pts[bi], W=W, ci=cb_idx: e.reduce_max(
                    colbuf[:, cbuf, ci:ci + 1], pt[:, 0:W], axis=AX),
                    reads=[ptoks[bi]], writes=[("colbuf", cbuf)],
                    cost=(130 + W) / 0.96)
                cb_idx += 1
            if has_ema:
                p.op("vector", lambda e, pe_t=pe_t: e.tensor_copy(ecol16[:, m:m + 1], pe_t[:, 0:1]),
                     reads=[tk_e], writes=[("ecol", m)])
        p.op("vector", lambda e, n=cb_idx: e.reduce_max(
            sraw16[:, m:m + 1], colbuf[:, cbuf, 0:n], axis=AX),
            reads=[("colbuf", cbuf)], writes=[("sraw", m)])

    def keygroup(gi, t0, n):
        g2 = gi % 2
        tiles = order[t0: t0 + n]
        for i, tid in enumerate(tiles):
            emit_key_tile(gi, i, tid, (GS * gi + i) % 10)
        p.op("scalar", lambda e, g2=g2: e.activation(
            rtmp8[:, g2, :].unsqueeze(2), mvg[:, g2, :, 1:2], AF.Ln, bias=constf[:, 0:1]),
            reads=[("mv", g2, i) for i in range(GS)] + [("constf",)],
            writes=[("rtmp8", g2)], cost=310.0, act_set="lnexp")
        p.op("scalar", lambda e, g2=g2: e.activation(
            rstd8[:, g2, :], rtmp8[:, g2, :], AF.Exp, scale=-0.5),
            reads=[("rtmp8", g2)], writes=[("rstd8", g2)], cost=310.0, act_set="lnexp")
        p.op("vector", lambda e, g2=g2: e.scalar_tensor_tensor(
            mb8[:, g2, :].unsqueeze(2), mvg[:, g2, :, 0:1], -1.0,
            rstd8[:, g2, :].unsqueeze(2), ALU.mult, ALU.mult),
            reads=[("rstd8", g2)] + [("mv", g2, i) for i in range(GS)],
            writes=[("mb8", g2)], cost=160.0)
        for i, tid in enumerate(tiles):
            emit_key_tile_b(gi, i, tid, (GS * gi + i) % 10)
        p.op("vector", lambda e, g2=g2: e.tensor_scalar(
            ss8[:, g2, :], ss8[:, g2, :], 1e-24, None, ALU.max),
            reads=[("ss", g2, i) for i in range(GS)],
            writes=[("ss", g2, i) for i in range(GS)])
        p.op("scalar", lambda e, g2=g2: e.activation(
            rtmp8[:, g2, :], ss8[:, g2, :], AF.Ln),
            reads=[("ss", g2, i) for i in range(GS)], writes=[("rtmp8", g2)],
            cost=310.0, act_set="lnexp")
        p.op("scalar", lambda e, g2=g2: e.activation(
            inv8[:, g2, :], rtmp8[:, g2, :], AF.Exp, scale=-0.5),
            reads=[("rtmp8", g2)], writes=[("inv8", g2)], cost=310.0, act_set="lnexp")
        for j in range(n // 2):
            emit_key_tile_c(gi, 2 * j, tiles[2 * j], (t0 + 2 * j) % 10)
            emit_key_tile_c(gi, 2 * j + 1, tiles[2 * j + 1], (t0 + 2 * j + 1) % 10)
            qk_slot((t0 + 2 * j) // 2)


    def gates_batch(g4):
        lo, hi = 4 * g4, 4 * g4 + 4
        srd = [("sraw", m) for m in range(lo, hi)] + [("ecol", m) for m in range(lo, hi)]
        gt = gtmp16[:, lo:hi]
        ec = ecol16[:, lo:hi]
        if g4 == 0:
            p.op("vector", lambda e: e.tensor_tensor(gt, sraw16[:, lo:hi], ec, ALU.subtract),
                 reads=srd, writes=[("gtmp", g4)])
            p.op("vector", lambda e: e.tensor_tensor(gt, gt, wm16[:, lo:hi], ALU.mult),
                 reads=[("gtmp", g4), ("wm16",)], writes=[("gtmp", g4)])
            p.op("vector", lambda e: e.tensor_tensor(gt, gt, ec, ALU.add),
                 reads=[("gtmp", g4)] + [("ecol", m) for m in range(lo, hi)],
                 writes=[("gtmp", g4)])
            p.op("vector", lambda e: e.tensor_scalar(gt, gt, A_col, None, ALU.mult),
                 reads=[("gtmp", g4), ("constcols",)], writes=[("gtmp", g4)])
        else:
            p.op("vector", lambda e: e.tensor_scalar(gt, sraw16[:, lo:hi], A_col, None, ALU.mult),
                 reads=srd + [("constcols",)], writes=[("gtmp", g4)])
        p.op("vector", lambda e: e.scalar_tensor_tensor(
            gt, ec, B_col, gt, ALU.mult, ALU.add),
            reads=[("gtmp", g4), ("constcols",)] + [("ecol", m) for m in range(lo, hi)],
            writes=[("gtmp", g4)])
        p.op("scalar", lambda e: e.activation(gate16[:, lo:hi], gt, AF.Exp, scale=-1.0),
             reads=[("gtmp", g4)], writes=[("gates", g4)], cost=310.0, act_set="lnexp")
        p.op("vector", lambda e: e.tensor_scalar(gate16[:, lo:hi], gate16[:, lo:hi],
                                                 al_col, be_col, ALU.mult, ALU.add),
             reads=[("gates", g4), ("constcols",)], writes=[("gates", g4)])

    def g_slot(m):
        i = m % 10
        buf = m % 2
        if m % 2 == 0:
            p.op("scalar", lambda e, m=m, i=i: e.dma_start(
                xbuf[:, i:i + 2, :],
                xd[m * 128:(m + 2) * 128, :].rearrange("(t q) d -> q t d", q=128)),
                writes=[("xbuf", i), ("xbuf", i + 1)], dma=True, cost=3700.0)
        p.op("vector", lambda e, m=m, i=i: e.scalar_tensor_tensor(
            h[:, m, :], h[:, m, :], gate16[:, m:m + 1], xbuf[:, i, :],
            ALU.mult, ALU.add),
            reads=[("h", m), ("gates", m // 4), ("xbuf", i)], writes=[("h", m)],
            cost=1230.0)
        p.op("vector", [
            lambda e, m=m: e.bn_stats(st26[:, m % 2, 0:6], h[:, m, 0:512]),
            lambda e, m=m: e.bn_stats(st26[:, m % 2, 6:12], h[:, m, 512:1024]),
        ], reads=[("h", m)], writes=[("st26", m % 2)], cost=1250.0)
        p.op("vector", lambda e, m=m: e.bn_aggr(mv2[:, m, :], st26[:, m % 2, :]),
             reads=[("st26", m % 2)], writes=[("mv2", m)], cost=90.0)
        p.op("scalar", lambda e, m=m: e.activation(
            rtmp16[:, m:m + 1], mv2[:, m, 1:2], AF.Ln, bias=constf[:, 0:1]),
            reads=[("mv2", m), ("constf",)], writes=[("rtmp16", m)],
            cost=300.0, act_set="lnexp")
        p.op("scalar", lambda e, m=m: e.activation(
            rstd16[:, m:m + 1], rtmp16[:, m:m + 1], AF.Exp, scale=-0.5),
            reads=[("rtmp16", m)], writes=[("rstd16", m)], cost=300.0, act_set="lnexp")

    t0 = 0
    for gi, n in enumerate([GS] * (NTILE // GS)):
        keygroup(gi, t0, n)
        t0 += n

    for g4 in range(4):
        gates_batch(g4)
        for m in range(4 * g4, 4 * g4 + 4):
            g_slot(m)

    def ffn_part1(m):
        buf = m % 2
        p.op("vector", lambda e: e.scalar_tensor_tensor(
            rtmp16[:, m:m + 1], mv2[:, m, 0:1], -1.0, rstd16[:, m:m + 1],
            ALU.mult, ALU.mult),
            reads=[("mv2", m), ("rstd16", m)], writes=[("rtmp16", m)], cost=90.0)
        p.op("scalar", lambda e: e.activation(
            z2tok[:, buf, :], h[:, m, :], AF.Identity,
            bias=rtmp16[:, m:m + 1], scale=rstd16[:, m:m + 1]),
            reads=[("h", m), ("rtmp16", m), ("rstd16", m)], writes=[("z2", buf)],
            cost=700.0, act_set="lnexp")
        pa, ta = psalloc("tr")
        pb, tb = psalloc("tr")
        fns = []
        for c in range(NCHUNK):
            ps = pa if c < 4 else pb
            fns.append(lambda e, c=c, ps=ps: e.matmul(
                ps[:, (c % 4) * 128:(c % 4) * 128 + 128],
                z2tok[:, buf, c * 128:(c + 1) * 128], ident[:, :],
                start=(c % 4 == 0), stop=(c % 4 == 3), skip_group_check=True))
        p.op("tensor", fns, reads=[("z2", buf), ("ident",)], writes=[ta, tb],
             cost=900.0)
        col = 128 * m
        p.op("scalar", lambda e: e.copy(
            kT[:, 0:4, col:col + 128], pa[:, :].rearrange("p (c f) -> p c f", c=4)),
            reads=[ta], writes=[("kT", m, 0)], cost=580.0)
        p.op("scalar", lambda e: e.copy(
            kT[:, 4:8, col:col + 128], pb[:, :].rearrange("p (c f) -> p c f", c=4)),
            reads=[tb], writes=[("kT", m, 1)], cost=580.0)
        z2toks = [("kT", m, 0), ("kT", m, 1)]
        psA, tA = psalloc("big")
        psB, tB = psalloc("big")
        psT, tT = psalloc("one")
        fnsT = []
        fnsAB = []
        for c in range(NCHUNK):
            lhs = kT[:, c, 128 * m:128 * m + 128]
            fnsT.append(lambda e, c=c, lhs=lhs, psT=psT: e.matmul(
                psT[:, 0:128], lhs, uTb[:, c, :],
                start=(c == 0), stop=(c == 7), skip_group_check=True))
            fnsT.append(lambda e, c=c, lhs=lhs, psT=psT: e.matmul(
                psT[:, 128:136], lhs, gwTb[:, c, :],
                start=False, stop=(c == 7), skip_group_check=True))
            fnsAB.append(lambda e, c=c, lhs=lhs, psA=psA: e.matmul(
                psA[:, :], lhs, wTb[:, c, 0:512],
                start=(c == 0), stop=False, skip_group_check=True))
            fnsAB.append(lambda e, c=c, lhs=lhs, psB=psB: e.matmul(
                psB[:, :], lhs, wTb[:, c, 512:1024],
                start=(c == 0), stop=False, skip_group_check=True))
        p.op("tensor", fnsT, reads=z2toks + [("uTb",), ("gwTb",)],
             writes=[tT], cost=1350.0)
        p.op("tensor", fnsAB, reads=z2toks + [("wTb",)],
             writes=[tA, tB], cost=3550.0)
        p.op("vector", lambda e, psT=psT: e.reduce_max(
            nmax2[:, buf:buf + 1], psT[:, 128:136], axis=AX, negate=True),
            reads=[tT], writes=[("nmax", buf)])
        p.op("scalar", lambda e, psT=psT: e.activation(
            gexp2[:, buf, :], psT[:, 128:136], AF.Exp, bias=nmax2[:, buf:buf + 1]),
            reads=[tT, ("nmax", buf)], writes=[("gexp", buf)],
            cost=310.0, act_set="lnexp")
        p.op("vector", lambda e: e.reduce_sum(
            ssum2[:, buf:buf + 1], gexp2[:, buf, :], axis=AX),
            reads=[("gexp", buf)], writes=[("ssum", buf)])
        p.op("vector", lambda e: e.reciprocal(
            rinv2[:, buf:buf + 1], ssum2[:, buf:buf + 1]),
            reads=[("ssum", buf)], writes=[("rinv", buf)])
        p.op("vector", lambda e: e.tensor_scalar(
            gexp2[:, buf, :], gexp2[:, buf, :], rinv2[:, buf:buf + 1], None, ALU.mult),
            reads=[("gexp", buf), ("rinv", buf)], writes=[("gexp", buf)])
        p.op("vector", lambda e, psT=psT: e.tensor_tensor(
            tgb[:, buf, :].rearrange("p (ee r) -> p ee r", ee=8),
            psT[:, 0:128].rearrange("p (ee r) -> p ee r", ee=8),
            gexp2[:, buf, :].unsqueeze(2).broadcast_to([128, 8, 16]),
            ALU.mult),
            reads=[tT, ("gexp", buf)], writes=[("tg", buf)], cost=280.0)
        return (m, buf, psA, tA, psB, tB, psT, tT)

    def ffn_part2(st):
        m, buf, psA, tA, psB, tB, psT, tT = st
        psT2, tT2 = psalloc("tr")
        p.op("tensor", lambda e: e.matmul(
            psT2[:, 0:128], tgb[:, buf, :], ident[:, :],
            start=True, stop=True, skip_group_check=True),
            reads=[("tg", buf), ("ident",)], writes=[tT2])
        p.op("scalar", lambda e: e.copy(tgTb[:, buf, :], psT2[:, 0:128]),
             reads=[tT2], writes=[("tgT", buf)], cost=280.0)
        p.op("tensor", [
            lambda e: e.matmul(psA[:, :], tgTb[:, buf, :], vPb[:, 0:512],
                               start=False, stop=True, skip_group_check=True),
            lambda e: e.matmul(psB[:, :], tgTb[:, buf, :], vPb[:, 512:1024],
                               start=False, stop=True, skip_group_check=True),
        ], reads=[("tgT", buf), ("vPb",)], writes=[tA, tB], cost=560.0)
        p.op("vector", lambda e: e.scalar_tensor_tensor(
            h[:, m, 0:512], psA[:, :], 1.0, h[:, m, 0:512], ALU.bypass, ALU.add),
            reads=[tA, ("h", m)], writes=[("h", m)], cost=660.0)
        p.op("vector", lambda e: e.scalar_tensor_tensor(
            h[:, m, 512:1024], psB[:, :], 1.0, h[:, m, 512:1024], ALU.bypass, ALU.add),
            reads=[tB, ("h", m)], writes=[("h", m)], cost=660.0)
        if m % 2 == 1:
            p.op("sync", lambda e: e.dma_start(
                yd[(m - 1) * 128:(m + 1) * 128, :].rearrange("(t q) d -> q t d", q=128),
                h[:, m - 1:m + 1, :]),
                reads=[("h", m - 1), ("h", m)], dma=True, cost=3700.0,
                prio_bonus=40000.0)

    prev = None
    for m in range(NSLOT):
        st = ffn_part1(m)
        if prev is not None:
            ffn_part2(prev)
        prev = st
    ffn_part2(prev)

    est = p.emit(nc, sems)
    ex.close()
    return nc, est


def kernel(L0_attn_x, L0_attn_ema_out, L0_attn_logit_decay, L0_attn_log_tau,
           L0_attn_log_blend, L0_attn_logit_blend_seq, L0_ffn_weight,
           L0_ffn_gate_w, L0_ffn_lora_U, L0_ffn_lora_V, **_unused):
    x = np.ascontiguousarray(np.asarray(L0_attn_x, dtype=np.float32))
    W = np.asarray(L0_ffn_weight, dtype=np.float32)
    gw = np.ascontiguousarray(np.asarray(L0_ffn_gate_w, dtype=np.float32))
    U = np.asarray(L0_ffn_lora_U, dtype=np.float32)
    V = np.asarray(L0_ffn_lora_V, dtype=np.float32)
    ema = np.ascontiguousarray(np.asarray(L0_attn_ema_out, dtype=np.float32).reshape(1, D))
    scal = np.array([[np.float32(L0_attn_log_tau), np.float32(L0_attn_log_blend),
                      np.float32(L0_attn_logit_blend_seq),
                      np.float32(L0_attn_logit_decay)]], dtype=np.float32)
    wT = np.ascontiguousarray(W.T)
    uT = np.ascontiguousarray(U.reshape(128, D).T)
    vP = np.ascontiguousarray(V.transpose(0, 2, 1).reshape(128, D))
    ident = np.eye(128, dtype=ml_dtypes.bfloat16)
    ii, jj = np.indices((128, 128))
    maskM = np.where(jj >= ii, MASKVAL, 0.0).astype(ml_dtypes.bfloat16)
    maskO1 = np.where(jj >= ii + 1, MASKVAL, 0.0).astype(ml_dtypes.bfloat16)
    wm0 = np.ones((128, NSLOT), np.float32)
    wm0[0, 0] = 0.0
    wm1 = np.ones((128, NSLOT), np.float32)

    nc, _est = build_program()
    in_maps = []
    for c in range(8):
        b, rho = c // 2, c % 2
        xg = np.ascontiguousarray(
            np.concatenate([x[b, rho::2], x[b, 1 - rho::2]], axis=0))
        in_maps.append({
            "x": xg, "wt": wT, "ut": uT, "vp": vP, "gw": gw, "ema": ema,
            "scal": scal, "maskm": maskM,
            "masko": maskM if rho == 0 else maskO1,
            "wm16": wm0 if rho == 0 else wm1,
            "ident": ident,
        })
    res = run_bass_kernel_spmd(nc, in_maps, core_ids=list(range(8)))
    out = np.empty((4, T, D), dtype=np.float32)
    for c in range(8):
        b, rho = c // 2, c % 2
        out[b, rho::2, :] = res.results[c]["y"]
    return out


if __name__ == "__main__":
    nc, est = build_program()
    n = sum(1 for _ in nc.all_instructions())
    print("built; instructions:", n, " scheduler-estimated span:", est / 1e3, "us")


# version 46
# speedup vs baseline: 1.0595x; 1.0008x over previous
import contextlib

import numpy as np
import ml_dtypes

import concourse.bass as bass
import concourse.mybir as mybir
from concourse.bass_utils import run_bass_kernel_spmd

F32 = mybir.dt.float32
BF16 = mybir.dt.bfloat16
AF = mybir.ActivationFunctionType
ALU = mybir.AluOpType
AX = mybir.AxisListType.X

D = 1024
T = 4096
NSLOT = 16
NCHUNK = 8
NTILE = 32
GS = 4
EPS = 1e-5
MASKVAL = -30000.0

ENGINES = ("sync", "tensor", "vector", "scalar", "gpsimd")
ESEM = {"tensor": "pe", "vector": "dve", "scalar": "act", "gpsimd": "pool",
        "sync": "spe"}
QPOOL = {"sync": [f"qs{i}" for i in range(8)],
         "scalar": [f"qc{i}" for i in range(4)]}
ALLSEMS = ["pe", "dve", "act", "pool", "spe"] + QPOOL["sync"] + QPOOL["scalar"]


class Op:
    __slots__ = ("oid", "engine", "fns", "dma", "cost", "act_set", "preds",
                 "succs", "prio", "start", "finish", "semn", "semv", "bonus")

    def __init__(self, oid, engine, fns, dma, cost, act_set):
        self.oid = oid
        self.engine = engine
        self.fns = fns
        self.dma = dma
        self.cost = cost
        self.act_set = act_set
        self.preds = set()
        self.succs = []
        self.prio = 0.0
        self.start = 0.0
        self.finish = 0.0
        self.semn = None
        self.semv = 0
        self.bonus = 0.0


ACT_SWITCH_NS = 1400.0


class P:

    def __init__(self):
        self.ops = []
        self.lastw = {}
        self.readers = {}
        self.default_bonus = 0.0

    def op(self, engine, fns, reads=(), writes=(), dma=False, cost=150.0,
           act_set=None, prio_bonus=0.0):
        if callable(fns):
            fns = [fns]
        o = Op(len(self.ops), engine, fns, dma, float(cost), act_set)
        o.bonus = float(prio_bonus) if prio_bonus else self.default_bonus
        for t in reads:
            w = self.lastw.get(t)
            if w is not None:
                o.preds.add(w)
        for t in writes:
            w = self.lastw.get(t)
            if w is not None:
                o.preds.add(w)
            for r in self.readers.get(t, ()):
                o.preds.add(r)
        o.preds.discard(o.oid)
        for t in writes:
            self.lastw[t] = o.oid
            self.readers[t] = []
        for t in reads:
            self.readers.setdefault(t, []).append(o.oid)
        self.ops.append(o)
        return o.oid

    def schedule(self):
        import heapq
        ops = self.ops
        for o in ops:
            for pid in o.preds:
                ops[pid].succs.append(o.oid)

        def blevel(scale):
            prio = [0.0] * len(ops)
            for o in reversed(ops):
                m = 0.0
                for s in o.succs:
                    if prio[s] > m:
                        m = prio[s]
                prio[o.oid] = o.cost * scale.get(o.engine, 1.0) + o.bonus + m
            return prio

        def greedy(prio, simscale):
            indeg = [len(o.preds) for o in ops]
            ready = {e: [] for e in ENGINES}
            data_ready = [0.0] * len(ops)
            finish = [0.0] * len(ops)
            for o in ops:
                if indeg[o.oid] == 0:
                    heapq.heappush(ready[o.engine], (-prio[o.oid], o.oid))
            engine_free = {e: 0.0 for e in ENGINES}
            queue_free = {"sync": 0.0, "scalar": 0.0}
            act_cur = None
            sched = {e: [] for e in ENGINES}
            n_left = len(ops)
            while n_left:
                best = None
                for e in ENGINES:
                    if not ready[e]:
                        continue
                    nprio, oid = ready[e][0]
                    o = ops[oid]
                    st = max(engine_free[e], data_ready[oid])
                    if o.dma:
                        st = max(st, queue_free[e])
                    if e == "scalar" and o.act_set and act_cur and \
                            o.act_set != act_cur:
                        st += ACT_SWITCH_NS
                    if best is None or st < best[0] or \
                            (st == best[0] and nprio < best[1]):
                        best = (st, nprio, e, oid)
                st, _, e, oid = best
                heapq.heappop(ready[e])
                o = ops[oid]
                finish[oid] = st + o.cost * simscale.get(e, 1.0)
                engine_free[e] = finish[oid] if not o.dma else st + 100.0
                if o.dma:
                    queue_free[e] = finish[oid]
                if e == "scalar" and o.act_set:
                    act_cur = o.act_set
                sched[e].append(oid)
                n_left -= 1
                for s in o.succs:
                    if finish[oid] > data_ready[s]:
                        data_ready[s] = finish[oid]
                    indeg[s] -= 1
                    if indeg[s] == 0:
                        heapq.heappush(ready[ops[s].engine], (-prio[s], s))
            return max(finish), sched

        est, sched = greedy(blevel({"vector": 2.5, "scalar": 1.35}),
                            {"vector": 1.25})
        self.sched = sched
        return est

    def emit(self, nc, sems):
        est = self.schedule()
        ops = self.ops
        cnt = {s: 0 for s in ALLSEMS}
        qidx = {"sync": 0, "scalar": 0}
        for e in ENGINES:
            for oid in self.sched[e]:
                o = ops[oid]
                if o.dma:
                    pool = QPOOL[e]
                    o.semn = pool[qidx[e] % len(pool)]
                    qidx[e] += 1
                    cnt[o.semn] += 16
                else:
                    o.semn = ESEM[e]
                    cnt[o.semn] += 1
                o.semv = cnt[o.semn]
        totals = dict(cnt)
        with nc.Block() as block:
            for ename in ENGINES:
                def section(eng, ename=ename):
                    clk = {}
                    for oid in self.sched[ename]:
                        o = ops[oid]
                        waits = {}
                        for pid in o.preds:
                            p_ = ops[pid]
                            if clk.get(p_.semn, 0) < p_.semv:
                                if waits.get(p_.semn, 0) < p_.semv:
                                    waits[p_.semn] = p_.semv
                        if o.dma and o.semv > 16:
                            prev = o.semv - 16
                            if clk.get(o.semn, 0) < prev and \
                                    waits.get(o.semn, 0) < prev:
                                waits[o.semn] = prev
                        for s, v in sorted(waits.items()):
                            eng.wait_ge(sems[s], v)
                            if clk.get(s, 0) < v:
                                clk[s] = v
                        inst = None
                        for fn in o.fns:
                            inst = fn(eng)
                        if inst is not None:
                            inst.then_inc(sems[o.semn], 16 if o.dma else 1)
                    if ename == "gpsimd":
                        for s in QPOOL["sync"] + QPOOL["scalar"]:
                            if totals.get(s, 0) > 0:
                                eng.wait_ge(sems[s], totals[s])
                getattr(block, ename)(section)
        return est


def build_program():
    nc = bass.Bass("TRN2", target_bir_lowering=False)

    xd = nc.dram_tensor("x", [T, D], F32, kind="ExternalInput").ap()
    wtd = nc.dram_tensor("wt", [D, D], F32, kind="ExternalInput").ap()
    utd = nc.dram_tensor("ut", [D, 128], F32, kind="ExternalInput").ap()
    vpd = nc.dram_tensor("vp", [128, D], F32, kind="ExternalInput").ap()
    gwd = nc.dram_tensor("gw", [8, D], F32, kind="ExternalInput").ap()
    emad = nc.dram_tensor("ema", [1, D], F32, kind="ExternalInput").ap()
    scald = nc.dram_tensor("scal", [1, 4], F32, kind="ExternalInput").ap()
    mmd = nc.dram_tensor("maskm", [128, 128], BF16, kind="ExternalInput").ap()
    mod = nc.dram_tensor("masko", [128, 128], BF16, kind="ExternalInput").ap()
    wmd = nc.dram_tensor("wm16", [128, NSLOT], F32, kind="ExternalInput").ap()
    idd = nc.dram_tensor("ident", [128, 128], BF16, kind="ExternalInput").ap()
    yd = nc.dram_tensor("y", [NSLOT * 128, D], F32, kind="ExternalOutput").ap()

    ex = contextlib.ExitStack()
    sb = lambda name, shape, dt: ex.enter_context(nc.sbuf_tensor(name, shape, dt))

    kT = sb("kT", [128, NCHUNK, T], BF16)
    h = sb("h", [128, NSLOT, D], F32)
    xbuf = sb("xbuf", [128, 10, D], F32)
    ztile = sb("ztile", [128, 1, D], F32)
    ktok = sb("ktok", [128, 2, D], BF16)
    z2tok = sb("z2tok", [128, 2, D], BF16)
    wTb = sb("wTb", [128, NCHUNK, D], BF16)
    uTb = sb("uTb", [128, NCHUNK, 128], BF16)
    gwTb = sb("gwTb", [128, NCHUNK, 8], BF16)
    vPb = sb("vPb", [128, D], BF16)

    ident = sb("identb", [128, 128], BF16)
    maskm = sb("maskmb", [128, 128], BF16)
    masko = sb("maskob", [128, 128], BF16)
    wm16 = sb("wm16b", [128, NSLOT], F32)
    emaf = sb("emaf", [128, NCHUNK], BF16)
    srow = sb("srow", [1, 8], F32)
    crow = sb("crow", [1, 8], F32)
    onesrow = sb("onesrow", [1, 128], F32)
    constcols = sb("constcols", [128, 8], F32)
    tiny = sb("tiny", [1, 8], F32)
    st6 = sb("st6", [128, 2, GS, 12], F32)
    mvg = sb("mvg", [128, 2, GS, 2], F32)
    rstd8 = sb("rstd8", [128, 2, GS], F32)
    rtmp8 = sb("rtmp8", [128, 2, GS], F32)
    ss8 = sb("ss8", [128, 2, GS], F32)
    inv8 = sb("inv8", [128, 2, GS], F32)
    mb8 = sb("mb8", [128, 2, GS], F32)
    colbuf = sb("colbuf", [128, 2, 16], F32)
    sraw16 = sb("sraw16", [128, NSLOT], F32)
    ecol16 = sb("ecol16", [128, NSLOT], F32)
    gtmp16 = sb("gtmp16", [128, NSLOT], F32)
    gate16 = sb("gate16", [128, NSLOT], F32)
    mv2 = sb("mv2", [128, NSLOT, 2], F32)
    st26 = sb("st26", [128, 2, 12], F32)
    rstd16 = sb("rstd16", [128, NSLOT], F32)
    rtmp16 = sb("rtmp16", [128, NSLOT], F32)
    nmax2 = sb("nmax2", [128, 2], F32)
    ssum2 = sb("ssum2", [128, 2], F32)
    rinv2 = sb("rinv2", [128, 2], F32)
    gexp2 = sb("gexp2", [128, 2, 8], F32)
    tgb = sb("tgb", [128, 2, 128], BF16)
    tgTb = sb("tgTb", [128, 2, 128], BF16)
    constf = sb("constf", [128, 2], F32)
    identf = sb("identf", [128, 128], F32)

    pst = [ex.enter_context(nc.psum_tensor(f"ps{i}", [128, 512], F32))
           for i in range(8)]
    pspools = {"big": [0, 1, 2, 3, 4], "tr": [6, 7], "one": [5]}
    psidx = {"big": 0, "tr": 0, "one": 0}

    def psalloc(pool="big"):
        lst = pspools[pool]
        i = lst[psidx[pool] % len(lst)]
        psidx[pool] += 1
        return pst[i], ("ps", i)

    sems = {s: ex.enter_context(nc.semaphore(s)) for s in ALLSEMS}

    wstage = xbuf
    emarow = ztile[0:1, 0, :]
    escr = xbuf[0:1, 0, :]
    embf = ktok[0:1, 0, :]
    gwbf = ktok[0:8, 1, :]

    p = P()

    p.op("gpsimd", lambda e: e.memset(constf[:, 0:1], EPS), writes=[("constf",)])
    p.op("gpsimd", lambda e: e.memset(constf[:, 1:2], 1e-24), writes=[("constf",)])
    p.op("scalar", lambda e: e.dma_start(maskm[:, :], mmd), writes=[("maskm",)], dma=True)
    p.op("scalar", lambda e: e.dma_start(masko[:, :], mod), writes=[("masko",)], dma=True)
    p.op("scalar", lambda e: e.dma_start(wm16[:, :], wmd), writes=[("wm16",)], dma=True)
    p.op("scalar", lambda e: e.dma_start(ident[:, :], idd), writes=[("ident",)], dma=True)
    p.op("vector", lambda e: e.tensor_copy(identf[:, :], ident[:, :]),
         reads=[("ident",)], writes=[("identf",)], cost=200.0)
    p.op("scalar", lambda e: e.dma_start(srow[:, 0:4], scald), writes=[("srow",)], dma=True)
    p.op("scalar", lambda e: e.dma_start(emarow, emad), writes=[("zt", 0)], dma=True)
    p.op("scalar", lambda e: e.dma_start(wstage[0:8, 1, :], gwd),
         writes=[("xbuf", 1)], dma=True)

    p.default_bonus = 60000.0
    p.op("scalar", lambda e: e.activation(crow[:, 4:5], srow[:, 0:1], AF.Exp),
         reads=[("srow",)], writes=[("crow",)])
    p.op("scalar", lambda e: e.activation(crow[:, 5:6], srow[:, 1:2], AF.Exp, scale=-1.0),
         reads=[("srow",)], writes=[("crow",)])
    p.op("scalar", lambda e: e.activation(crow[:, 6:7], srow[:, 2:3], AF.Exp, scale=-1.0),
         reads=[("srow",)], writes=[("crow",)])
    p.op("vector", lambda e: e.tensor_scalar(crow[:, 5:6], crow[:, 5:6], 1.0, None, ALU.add),
         reads=[("crow",)], writes=[("crow",)])
    p.op("vector", lambda e: e.reciprocal(crow[:, 2:3], crow[:, 5:6]),
         reads=[("crow",)], writes=[("crow",)])
    p.op("vector", lambda e: e.tensor_scalar(crow[:, 3:4], crow[:, 2:3], -1.0, 1.0, ALU.mult, ALU.add),
         reads=[("crow",)], writes=[("crow",)])
    p.op("vector", lambda e: e.tensor_scalar(crow[:, 6:7], crow[:, 6:7], 1.0, None, ALU.add),
         reads=[("crow",)], writes=[("crow",)])
    p.op("vector", lambda e: e.reciprocal(crow[:, 7:8], crow[:, 6:7]),
         reads=[("crow",)], writes=[("crow",)])
    p.op("vector", lambda e: e.tensor_tensor(crow[:, 0:1], crow[:, 4:5], crow[:, 7:8], ALU.mult),
         reads=[("crow",)], writes=[("crow",)])
    p.op("vector", lambda e: e.tensor_scalar(crow[:, 6:7], crow[:, 7:8], -1.0, 1.0, ALU.mult, ALU.add),
         reads=[("crow",)], writes=[("crow",)])
    p.op("vector", lambda e: e.tensor_tensor(crow[:, 1:2], crow[:, 4:5], crow[:, 6:7], ALU.mult),
         reads=[("crow",)], writes=[("crow",)])
    p.op("vector", lambda e: e.memset(onesrow[:, :], 1.0), writes=[("ones",)])
    psC, tC = psalloc("big")
    p.op("tensor", lambda e: e.matmul(psC[:, 0:8], onesrow[:, :], crow[:, :],
                                      start=True, stop=True),
         reads=[("ones",), ("crow",)], writes=[tC])
    p.op("vector", lambda e: e.tensor_copy(constcols[:, :], psC[:, 0:8]),
         reads=[tC], writes=[("constcols",)])
    A_col = constcols[:, 0:1]
    B_col = constcols[:, 1:2]
    al_col = constcols[:, 2:3]
    be_col = constcols[:, 3:4]

    p.op("scalar", lambda e: e.activation(escr, emarow, AF.Square,
                                          accum_out=tiny[:, 0:1]),
         reads=[("zt", 0)], writes=[("xbuf", 0), ("tiny",)])
    p.op("vector", lambda e: e.tensor_scalar(tiny[:, 0:1], tiny[:, 0:1], 1e-24, None, ALU.max),
         reads=[("tiny",)], writes=[("tiny",)])
    p.op("scalar", lambda e: e.activation(tiny[:, 1:2], tiny[:, 0:1], AF.Ln),
         reads=[("tiny",)], writes=[("tiny",)])
    p.op("scalar", lambda e: e.activation(tiny[:, 2:3], tiny[:, 1:2], AF.Exp, scale=-0.5),
         reads=[("tiny",)], writes=[("tiny",)])
    p.op("vector", lambda e: e.tensor_scalar(embf, emarow, tiny[:, 2:3], None, ALU.mult),
         reads=[("zt", 0), ("tiny",)], writes=[("ktok", 0)])
    psE, tE = psalloc("big")
    fns = []
    for c in range(NCHUNK):
        fns.append(lambda e, c=c: e.matmul(psE[:, c:c + 1], embf[:, c * 128:(c + 1) * 128],
                                           ident[0:1, 0:1], start=(c == 0), stop=(c == 7),
                                           skip_group_check=True))
    p.op("tensor", fns, reads=[("ktok", 0), ("ident",)], writes=[tE])
    p.op("vector", lambda e: e.tensor_copy(emaf[:, :], psE[:, 0:8]),
         reads=[tE], writes=[("emaf",)])

    p.op("vector", lambda e: e.tensor_copy(gwbf, wstage[0:8, 1, :]),
         reads=[("xbuf", 1)], writes=[("ktok", 1)])
    psG, tG = psalloc("big")
    fns = []
    for c in range(NCHUNK):
        fns.append(lambda e, c=c: e.matmul(psG[:, c * 8:(c + 1) * 8],
                                           gwbf[:, c * 128:(c + 1) * 128],
                                           ident[0:8, 0:8], start=(c == 0), stop=(c == 7),
                                           skip_group_check=True))
    p.op("tensor", fns, reads=[("ktok", 1), ("ident",)], writes=[tG])
    p.op("vector", lambda e: e.tensor_copy(
        gwTb[:, :, :], psG[:, 0:64].rearrange("p (c e) -> p c e", c=8)),
        reads=[tG], writes=[("gwTb",)])

    p.default_bonus = 0.0
    for c in range(NCHUNK):
        p.op("scalar", lambda e, c=c: e.dma_start(
            wstage[:, c % 2, :], wtd[c * 128:(c + 1) * 128, :]),
            writes=[("xbuf", c % 2)], dma=True, cost=2500.0)
        p.op("vector", lambda e, c=c: e.tensor_copy(wTb[:, c, :], wstage[:, c % 2, :]),
             reads=[("xbuf", c % 2)], writes=[("wTb",)], cost=600.0)
    p.op("scalar", lambda e: e.dma_start(
        wstage[:, 0, :].rearrange("p (c j) -> p c j", c=8),
        utd.rearrange("(c q) j -> q c j", q=128)),
        writes=[("xbuf", 0)], dma=True)
    p.op("vector", lambda e: e.tensor_copy(
        uTb[:, :, :], wstage[:, 0, :].rearrange("p (c j) -> p c j", c=8)),
        reads=[("xbuf", 0)], writes=[("uTb",)])
    p.op("scalar", lambda e: e.dma_start(wstage[:, 1, :], vpd),
         writes=[("xbuf", 1)], dma=True)
    p.op("vector", lambda e: e.tensor_scalar(vPb[:, :], wstage[:, 1, :], 0.0625, None, ALU.mult),
         reads=[("xbuf", 1)], writes=[("vPb",)])

    order = []
    for j in range(NSLOT):
        order.append(j)
        order.append(16 + j)

    def kcol(tid):
        return 2048 * (tid // 16) + 128 * (tid % 16)

    xview = xd.rearrange("(g t q) d -> q g t d", g=2, q=128)

    def emit_key_tile(gi, i, tid, ri):
        g2 = gi % 2
        if ri % 2 == 0:
            j = tid
            p.op("sync", lambda e, ri=ri, j=j: e.dma_start(
                xbuf[:, ri:ri + 2, :], xview[:, :, j, :]),
                writes=[("xbuf", ri), ("xbuf", ri + 1)], dma=True, cost=3700.0)
        p.op("vector", [
            lambda e: e.bn_stats(st6[:, g2, i, 0:6], xbuf[:, ri, 0:512]),
            lambda e: e.bn_stats(st6[:, g2, i, 6:12], xbuf[:, ri, 512:1024]),
        ], reads=[("xbuf", ri)], writes=[("st6", g2, i)], cost=1250.0)
        p.op("vector", lambda e: e.bn_aggr(mvg[:, g2, i, :], st6[:, g2, i, :]),
             reads=[("st6", g2, i)], writes=[("mv", g2, i)], cost=90.0)

    def emit_key_tile_b(gi, i, tid, ri):
        g2 = gi % 2
        buf = ri % 2
        mine = tid < 16
        gdst = h[:, tid, :] if mine else xbuf[:, ri, :]
        gtok = ("h", tid) if mine else ("xbuf", ri)
        p.op("scalar", lambda e: e.activation(
            gdst, xbuf[:, ri, :], AF.Gelu_apprx_tanh,
            bias=mb8[:, g2, i:i + 1], scale=rstd8[:, g2, i:i + 1]),
            reads=[("xbuf", ri), ("rstd8", g2), ("mb8", g2)], writes=[gtok],
            cost=1150.0, act_set="gelu")
        scr = xbuf[:, ri, :] if mine else ztile[:, 0, :]
        stok = ("xbuf", ri) if mine else ("zt", 0)
        if tid % 2 == 0:
            p.op("scalar", lambda e: e.activation(
                scr, gdst, AF.Square, accum_out=ss8[:, g2, i:i + 1]),
                reads=[gtok], writes=[stok, ("ss", g2, i)], cost=1150.0)
        else:
            p.op("vector", lambda e: e.scalar_tensor_tensor(
                scr, gdst, 1.0, gdst, ALU.bypass, ALU.mult,
                accum_out=ss8[:, g2, i:i + 1]),
                reads=[gtok], writes=[stok, ("ss", g2, i)], cost=1230.0)

    def emit_key_tile_c(gi, i, tid, ri):
        g2 = gi % 2
        buf = ri % 2
        mine = tid < 16
        gdst = h[:, tid, :] if mine else xbuf[:, ri, :]
        gtok = ("h", tid) if mine else ("xbuf", ri)
        col = kcol(tid)
        p.op("gpsimd", lambda e: e.tensor_tensor(
            ktok[:, buf, :], gdst,
            inv8[:, g2, i:i + 1].broadcast_to([128, D]), ALU.mult),
            reads=[gtok, ("inv8", g2)], writes=[("ktok", buf)], cost=2300.0)
        pa, ta = psalloc("tr")
        pb, tb = psalloc("tr")
        fns = []
        for c in range(NCHUNK):
            ps = pa if c < 4 else pb
            fns.append(lambda e, c=c, ps=ps: e.matmul(
                ps[:, (c % 4) * 128:(c % 4) * 128 + 128],
                ktok[:, buf, c * 128:(c + 1) * 128], ident[:, :],
                start=(c % 4 == 0), stop=(c % 4 == 3), skip_group_check=True))
        p.op("tensor", fns, reads=[("ktok", buf), ("ident",)], writes=[ta, tb],
             cost=900.0)
        p.op("vector", lambda e: e.tensor_copy(
            kT[:, 0:4, col:col + 128],
            pa[:, :].rearrange("p (c f) -> p c f", c=4)),
            reads=[ta], writes=[("kT", tid, 0)], cost=660.0)
        p.op("scalar", lambda e: e.copy(
            kT[:, 4:8, col:col + 128],
            pb[:, :].rearrange("p (c f) -> p c f", c=4)),
            reads=[tb], writes=[("kT", tid, 1)], cost=580.0)

    def qk_slot(m):
        L = m + 1
        q, rem = divmod(L, 4)
        blocks = [[512 * i, 512, False] for i in range(q)]
        if rem:
            blocks.append([512 * q, 128 * rem, True])
        else:
            blocks[-1][2] = True
        allb = [("m", bs, W, iswin) for (bs, W, iswin) in blocks] + \
               [("o", bs, W, iswin) for (bs, W, iswin) in blocks]
        cbuf = m % 2
        cb_idx = 0
        groups = [allb[i:i + 2] for i in range(0, len(allb), 2)]
        qtok = [("kT", m, 0), ("kT", m, 1)]
        for gi2, grp in enumerate(groups):
            ptoks = []
            pts = []
            reads = list(qtok) + [("ident",)]
            for (side, bs, W, iswin) in grp:
                pt, tk = psalloc()
                pts.append(pt)
                ptoks.append(tk)
                base = 0 if side == "m" else 16
                for j in range(bs // 128, (bs + W - 1) // 128 + 1):
                    reads.append(("kT", base + j, 0))
                    reads.append(("kT", base + j, 1))
                if iswin:
                    reads.append(("maskm",) if side == "m" else ("masko",))
            has_ema = (gi2 == 0)
            if has_ema:
                pe_t, tk_e = psalloc("one")
                ptoks.append(tk_e)
                reads.append(("emaf",))
            fns = []
            for c in range(NCHUNK):
                for bi, (side, bs, W, iswin) in enumerate(grp):
                    off = (0 if side == "m" else 2048) + bs
                    fns.append(lambda e, c=c, pt=pts[bi], off=off, W=W, iswin=iswin: e.matmul(
                        pt[:, 0:W],
                        kT[:, c, 128 * m:128 * m + 128],
                        kT[:, c, off:off + W],
                        start=(c == 0), stop=(c == 7 and not iswin),
                        skip_group_check=True))
                if has_ema:
                    fns.append(lambda e, c=c, pe_t=pe_t: e.matmul(
                        pe_t[:, 0:1], kT[:, c, 128 * m:128 * m + 128],
                        emaf[:, c:c + 1],
                        start=(c == 0), stop=(c == 7), skip_group_check=True))
            for bi, (side, bs, W, iswin) in enumerate(grp):
                if iswin:
                    msk = maskm if side == "m" else masko
                    fns.append(lambda e, pt=pts[bi], W=W, msk=msk: e.matmul(
                        pt[:, W - 128:W], ident[:, :], msk[:, :],
                        start=False, stop=True, skip_group_check=True))
            mmcost = (8 * sum(W for (_, _, W, _) in grp)
                      + 128 * sum(1 for b in grp if b[3])) / 2.4 + 8 * 110.0
            p.op("tensor", fns, reads=reads, writes=ptoks, cost=mmcost)
            for bi, (side, bs, W, iswin) in enumerate(grp):
                p.op("vector", lambda e, pt=pts[bi], W=W, ci=cb_idx: e.reduce_max(
                    colbuf[:, cbuf, ci:ci + 1], pt[:, 0:W], axis=AX),
                    reads=[ptoks[bi]], writes=[("colbuf", cbuf)],
                    cost=(130 + W) / 0.96)
                cb_idx += 1
            if has_ema:
                p.op("vector", lambda e, pe_t=pe_t: e.tensor_copy(ecol16[:, m:m + 1], pe_t[:, 0:1]),
                     reads=[tk_e], writes=[("ecol", m)])
        p.op("vector", lambda e, n=cb_idx: e.reduce_max(
            sraw16[:, m:m + 1], colbuf[:, cbuf, 0:n], axis=AX),
            reads=[("colbuf", cbuf)], writes=[("sraw", m)])

    def keygroup(gi, t0, n):
        g2 = gi % 2
        tiles = order[t0: t0 + n]
        for i, tid in enumerate(tiles):
            emit_key_tile(gi, i, tid, (GS * gi + i) % 10)
        p.op("scalar", lambda e, g2=g2: e.activation(
            rtmp8[:, g2, :].unsqueeze(2), mvg[:, g2, :, 1:2], AF.Ln, bias=constf[:, 0:1]),
            reads=[("mv", g2, i) for i in range(GS)] + [("constf",)],
            writes=[("rtmp8", g2)], cost=310.0, act_set="lnexp")
        p.op("scalar", lambda e, g2=g2: e.activation(
            rstd8[:, g2, :], rtmp8[:, g2, :], AF.Exp, scale=-0.5),
            reads=[("rtmp8", g2)], writes=[("rstd8", g2)], cost=310.0, act_set="lnexp")
        p.op("vector", lambda e, g2=g2: e.scalar_tensor_tensor(
            mb8[:, g2, :].unsqueeze(2), mvg[:, g2, :, 0:1], -1.0,
            rstd8[:, g2, :].unsqueeze(2), ALU.mult, ALU.mult),
            reads=[("rstd8", g2)] + [("mv", g2, i) for i in range(GS)],
            writes=[("mb8", g2)], cost=160.0)
        for i, tid in enumerate(tiles):
            emit_key_tile_b(gi, i, tid, (GS * gi + i) % 10)
        p.op("vector", lambda e, g2=g2: e.tensor_scalar(
            ss8[:, g2, :], ss8[:, g2, :], 1e-24, None, ALU.max),
            reads=[("ss", g2, i) for i in range(GS)],
            writes=[("ss", g2, i) for i in range(GS)])
        p.op("scalar", lambda e, g2=g2: e.activation(
            rtmp8[:, g2, :], ss8[:, g2, :], AF.Ln),
            reads=[("ss", g2, i) for i in range(GS)], writes=[("rtmp8", g2)],
            cost=310.0, act_set="lnexp")
        p.op("scalar", lambda e, g2=g2: e.activation(
            inv8[:, g2, :], rtmp8[:, g2, :], AF.Exp, scale=-0.5),
            reads=[("rtmp8", g2)], writes=[("inv8", g2)], cost=310.0, act_set="lnexp")
        for j in range(n // 2):
            emit_key_tile_c(gi, 2 * j, tiles[2 * j], (t0 + 2 * j) % 10)
            emit_key_tile_c(gi, 2 * j + 1, tiles[2 * j + 1], (t0 + 2 * j + 1) % 10)
            qk_slot((t0 + 2 * j) // 2)


    def gates_batch(g4):
        lo, hi = 4 * g4, 4 * g4 + 4
        srd = [("sraw", m) for m in range(lo, hi)] + [("ecol", m) for m in range(lo, hi)]
        gt = gtmp16[:, lo:hi]
        ec = ecol16[:, lo:hi]
        if g4 == 0:
            p.op("vector", lambda e: e.tensor_tensor(gt, sraw16[:, lo:hi], ec, ALU.subtract),
                 reads=srd, writes=[("gtmp", g4)])
            p.op("vector", lambda e: e.tensor_tensor(gt, gt, wm16[:, lo:hi], ALU.mult),
                 reads=[("gtmp", g4), ("wm16",)], writes=[("gtmp", g4)])
            p.op("vector", lambda e: e.tensor_tensor(gt, gt, ec, ALU.add),
                 reads=[("gtmp", g4)] + [("ecol", m) for m in range(lo, hi)],
                 writes=[("gtmp", g4)])
            p.op("vector", lambda e: e.tensor_scalar(gt, gt, A_col, None, ALU.mult),
                 reads=[("gtmp", g4), ("constcols",)], writes=[("gtmp", g4)])
        else:
            p.op("vector", lambda e: e.tensor_scalar(gt, sraw16[:, lo:hi], A_col, None, ALU.mult),
                 reads=srd + [("constcols",)], writes=[("gtmp", g4)])
        p.op("vector", lambda e: e.scalar_tensor_tensor(
            gt, ec, B_col, gt, ALU.mult, ALU.add),
            reads=[("gtmp", g4), ("constcols",)] + [("ecol", m) for m in range(lo, hi)],
            writes=[("gtmp", g4)])
        p.op("scalar", lambda e: e.activation(gate16[:, lo:hi], gt, AF.Exp, scale=-1.0),
             reads=[("gtmp", g4)], writes=[("gates", g4)], cost=310.0, act_set="lnexp")
        p.op("vector", lambda e: e.tensor_scalar(gate16[:, lo:hi], gate16[:, lo:hi],
                                                 al_col, be_col, ALU.mult, ALU.add),
             reads=[("gates", g4), ("constcols",)], writes=[("gates", g4)])

    def g_slot(m):
        i = m % 10
        buf = m % 2
        if m % 2 == 0:
            p.op("scalar", lambda e, m=m, i=i: e.dma_start(
                xbuf[:, i:i + 2, :],
                xd[m * 128:(m + 2) * 128, :].rearrange("(t q) d -> q t d", q=128)),
                writes=[("xbuf", i), ("xbuf", i + 1)], dma=True, cost=3700.0)
        p.op("vector", lambda e, m=m, i=i: e.scalar_tensor_tensor(
            h[:, m, :], h[:, m, :], gate16[:, m:m + 1], xbuf[:, i, :],
            ALU.mult, ALU.add),
            reads=[("h", m), ("gates", m // 4), ("xbuf", i)], writes=[("h", m)],
            cost=1230.0)
        p.op("vector", [
            lambda e, m=m: e.bn_stats(st26[:, m % 2, 0:6], h[:, m, 0:512]),
            lambda e, m=m: e.bn_stats(st26[:, m % 2, 6:12], h[:, m, 512:1024]),
        ], reads=[("h", m)], writes=[("st26", m % 2)], cost=1250.0)
        p.op("vector", lambda e, m=m: e.bn_aggr(mv2[:, m, :], st26[:, m % 2, :]),
             reads=[("st26", m % 2)], writes=[("mv2", m)], cost=90.0)
        p.op("scalar", lambda e, m=m: e.activation(
            rtmp16[:, m:m + 1], mv2[:, m, 1:2], AF.Ln, bias=constf[:, 0:1]),
            reads=[("mv2", m), ("constf",)], writes=[("rtmp16", m)],
            cost=300.0, act_set="lnexp")
        p.op("scalar", lambda e, m=m: e.activation(
            rstd16[:, m:m + 1], rtmp16[:, m:m + 1], AF.Exp, scale=-0.5),
            reads=[("rtmp16", m)], writes=[("rstd16", m)], cost=300.0, act_set="lnexp")

    t0 = 0
    for gi, n in enumerate([GS] * (NTILE // GS)):
        keygroup(gi, t0, n)
        t0 += n

    for g4 in range(4):
        gates_batch(g4)
        for m in range(4 * g4, 4 * g4 + 4):
            g_slot(m)

    def ffn_part1(m):
        buf = m % 2
        p.op("vector", lambda e: e.scalar_tensor_tensor(
            rtmp16[:, m:m + 1], mv2[:, m, 0:1], -1.0, rstd16[:, m:m + 1],
            ALU.mult, ALU.mult),
            reads=[("mv2", m), ("rstd16", m)], writes=[("rtmp16", m)], cost=90.0)
        p.op("scalar", lambda e: e.activation(
            z2tok[:, buf, :], h[:, m, :], AF.Identity,
            bias=rtmp16[:, m:m + 1], scale=rstd16[:, m:m + 1]),
            reads=[("h", m), ("rtmp16", m), ("rstd16", m)], writes=[("z2", buf)],
            cost=700.0, act_set="lnexp")
        pa, ta = psalloc("tr")
        pb, tb = psalloc("tr")
        fns = []
        for c in range(NCHUNK):
            ps = pa if c < 4 else pb
            fns.append(lambda e, c=c, ps=ps: e.matmul(
                ps[:, (c % 4) * 128:(c % 4) * 128 + 128],
                z2tok[:, buf, c * 128:(c + 1) * 128], ident[:, :],
                start=(c % 4 == 0), stop=(c % 4 == 3), skip_group_check=True))
        p.op("tensor", fns, reads=[("z2", buf), ("ident",)], writes=[ta, tb],
             cost=900.0)
        col = 128 * m
        p.op("vector", lambda e: e.tensor_copy(
            kT[:, 0:4, col:col + 128], pa[:, :].rearrange("p (c f) -> p c f", c=4)),
            reads=[ta], writes=[("kT", m, 0)], cost=660.0)
        p.op("scalar", lambda e: e.copy(
            kT[:, 4:8, col:col + 128], pb[:, :].rearrange("p (c f) -> p c f", c=4)),
            reads=[tb], writes=[("kT", m, 1)], cost=580.0)
        z2toks = [("kT", m, 0), ("kT", m, 1)]
        psA, tA = psalloc("big")
        psB, tB = psalloc("big")
        psT, tT = psalloc("one")
        fnsT = []
        fnsAB = []
        for c in range(NCHUNK):
            lhs = kT[:, c, 128 * m:128 * m + 128]
            fnsT.append(lambda e, c=c, lhs=lhs, psT=psT: e.matmul(
                psT[:, 0:128], lhs, uTb[:, c, :],
                start=(c == 0), stop=(c == 7), skip_group_check=True))
            fnsT.append(lambda e, c=c, lhs=lhs, psT=psT: e.matmul(
                psT[:, 128:136], lhs, gwTb[:, c, :],
                start=False, stop=(c == 7), skip_group_check=True))
            fnsAB.append(lambda e, c=c, lhs=lhs, psA=psA: e.matmul(
                psA[:, :], lhs, wTb[:, c, 0:512],
                start=(c == 0), stop=False, skip_group_check=True))
            fnsAB.append(lambda e, c=c, lhs=lhs, psB=psB: e.matmul(
                psB[:, :], lhs, wTb[:, c, 512:1024],
                start=(c == 0), stop=False, skip_group_check=True))
        p.op("tensor", fnsT, reads=z2toks + [("uTb",), ("gwTb",)],
             writes=[tT], cost=1350.0)
        p.op("tensor", fnsAB, reads=z2toks + [("wTb",)],
             writes=[tA, tB], cost=3550.0)
        p.op("vector", lambda e, psT=psT: e.reduce_max(
            nmax2[:, buf:buf + 1], psT[:, 128:136], axis=AX, negate=True),
            reads=[tT], writes=[("nmax", buf)])
        p.op("scalar", lambda e, psT=psT: e.activation(
            gexp2[:, buf, :], psT[:, 128:136], AF.Exp, bias=nmax2[:, buf:buf + 1]),
            reads=[tT, ("nmax", buf)], writes=[("gexp", buf)],
            cost=310.0, act_set="lnexp")
        p.op("vector", lambda e: e.reduce_sum(
            ssum2[:, buf:buf + 1], gexp2[:, buf, :], axis=AX),
            reads=[("gexp", buf)], writes=[("ssum", buf)])
        p.op("vector", lambda e: e.reciprocal(
            rinv2[:, buf:buf + 1], ssum2[:, buf:buf + 1]),
            reads=[("ssum", buf)], writes=[("rinv", buf)])
        p.op("vector", lambda e: e.tensor_scalar(
            gexp2[:, buf, :], gexp2[:, buf, :], rinv2[:, buf:buf + 1], None, ALU.mult),
            reads=[("gexp", buf), ("rinv", buf)], writes=[("gexp", buf)])
        p.op("vector", lambda e, psT=psT: e.tensor_tensor(
            tgb[:, buf, :].rearrange("p (ee r) -> p ee r", ee=8),
            psT[:, 0:128].rearrange("p (ee r) -> p ee r", ee=8),
            gexp2[:, buf, :].unsqueeze(2).broadcast_to([128, 8, 16]),
            ALU.mult),
            reads=[tT, ("gexp", buf)], writes=[("tg", buf)], cost=280.0)
        return (m, buf, psA, tA, psB, tB, psT, tT)

    def ffn_part2(st):
        m, buf, psA, tA, psB, tB, psT, tT = st
        psT2, tT2 = psalloc("tr")
        p.op("tensor", lambda e: e.matmul(
            psT2[:, 0:128], tgb[:, buf, :], ident[:, :],
            start=True, stop=True, skip_group_check=True),
            reads=[("tg", buf), ("ident",)], writes=[tT2])
        p.op("scalar", lambda e: e.activation(
            tgTb[:, buf, :], psT2[:, 0:128], AF.Identity),
            reads=[tT2], writes=[("tgT", buf)], cost=280.0, act_set="lnexp")
        p.op("tensor", [
            lambda e: e.matmul(psA[:, :], tgTb[:, buf, :], vPb[:, 0:512],
                               start=False, stop=True, skip_group_check=True),
            lambda e: e.matmul(psB[:, :], tgTb[:, buf, :], vPb[:, 512:1024],
                               start=False, stop=True, skip_group_check=True),
        ], reads=[("tgT", buf), ("vPb",)], writes=[tA, tB], cost=560.0)
        p.op("vector", lambda e: e.scalar_tensor_tensor(
            h[:, m, 0:512], psA[:, :], 1.0, h[:, m, 0:512], ALU.bypass, ALU.add),
            reads=[tA, ("h", m)], writes=[("h", m)], cost=660.0)
        p.op("vector", lambda e: e.scalar_tensor_tensor(
            h[:, m, 512:1024], psB[:, :], 1.0, h[:, m, 512:1024], ALU.bypass, ALU.add),
            reads=[tB, ("h", m)], writes=[("h", m)], cost=660.0)
        if m % 2 == 1:
            p.op("sync", lambda e: e.dma_start(
                yd[(m - 1) * 128:(m + 1) * 128, :].rearrange("(t q) d -> q t d", q=128),
                h[:, m - 1:m + 1, :]),
                reads=[("h", m - 1), ("h", m)], dma=True, cost=3700.0,
                prio_bonus=40000.0)

    prev = None
    for m in range(NSLOT):
        st = ffn_part1(m)
        if prev is not None:
            ffn_part2(prev)
        prev = st
    ffn_part2(prev)

    est = p.emit(nc, sems)
    ex.close()
    return nc, est


def kernel(L0_attn_x, L0_attn_ema_out, L0_attn_logit_decay, L0_attn_log_tau,
           L0_attn_log_blend, L0_attn_logit_blend_seq, L0_ffn_weight,
           L0_ffn_gate_w, L0_ffn_lora_U, L0_ffn_lora_V, **_unused):
    x = np.ascontiguousarray(np.asarray(L0_attn_x, dtype=np.float32))
    W = np.asarray(L0_ffn_weight, dtype=np.float32)
    gw = np.ascontiguousarray(np.asarray(L0_ffn_gate_w, dtype=np.float32))
    U = np.asarray(L0_ffn_lora_U, dtype=np.float32)
    V = np.asarray(L0_ffn_lora_V, dtype=np.float32)
    ema = np.ascontiguousarray(np.asarray(L0_attn_ema_out, dtype=np.float32).reshape(1, D))
    scal = np.array([[np.float32(L0_attn_log_tau), np.float32(L0_attn_log_blend),
                      np.float32(L0_attn_logit_blend_seq),
                      np.float32(L0_attn_logit_decay)]], dtype=np.float32)
    wT = np.ascontiguousarray(W.T)
    uT = np.ascontiguousarray(U.reshape(128, D).T)
    vP = np.ascontiguousarray(V.transpose(0, 2, 1).reshape(128, D))
    ident = np.eye(128, dtype=ml_dtypes.bfloat16)
    ii, jj = np.indices((128, 128))
    maskM = np.where(jj >= ii, MASKVAL, 0.0).astype(ml_dtypes.bfloat16)
    maskO1 = np.where(jj >= ii + 1, MASKVAL, 0.0).astype(ml_dtypes.bfloat16)
    wm0 = np.ones((128, NSLOT), np.float32)
    wm0[0, 0] = 0.0
    wm1 = np.ones((128, NSLOT), np.float32)

    nc, _est = build_program()
    in_maps = []
    for c in range(8):
        b, rho = c // 2, c % 2
        xg = np.ascontiguousarray(
            np.concatenate([x[b, rho::2], x[b, 1 - rho::2]], axis=0))
        in_maps.append({
            "x": xg, "wt": wT, "ut": uT, "vp": vP, "gw": gw, "ema": ema,
            "scal": scal, "maskm": maskM,
            "masko": maskM if rho == 0 else maskO1,
            "wm16": wm0 if rho == 0 else wm1,
            "ident": ident,
        })
    res = run_bass_kernel_spmd(nc, in_maps, core_ids=list(range(8)))
    out = np.empty((4, T, D), dtype=np.float32)
    for c in range(8):
        b, rho = c // 2, c % 2
        out[b, rho::2, :] = res.results[c]["y"]
    return out


if __name__ == "__main__":
    nc, est = build_program()
    n = sum(1 for _ in nc.all_instructions())
    print("built; instructions:", n, " scheduler-estimated span:", est / 1e3, "us")


# version 48
# speedup vs baseline: 1.1329x; 1.0693x over previous
import contextlib

import numpy as np
import ml_dtypes

import concourse.bass as bass
import concourse.mybir as mybir
from concourse.bass_utils import run_bass_kernel_spmd

F32 = mybir.dt.float32
BF16 = mybir.dt.bfloat16
AF = mybir.ActivationFunctionType
ALU = mybir.AluOpType
AX = mybir.AxisListType.X

D = 1024
T = 4096
NSLOT = 16
NCHUNK = 8
NTILE = 32
GS = 4
EPS = 1e-5
MASKVAL = -30000.0

ENGINES = ("sync", "tensor", "vector", "scalar", "gpsimd")
ESEM = {"tensor": "pe", "vector": "dve", "scalar": "act", "gpsimd": "pool",
        "sync": "spe"}
QPOOL = {"sync": [f"qs{i}" for i in range(8)],
         "scalar": [f"qc{i}" for i in range(4)]}
ALLSEMS = ["pe", "dve", "act", "pool", "spe"] + QPOOL["sync"] + QPOOL["scalar"]


class Op:
    __slots__ = ("oid", "engine", "fns", "dma", "cost", "act_set", "preds",
                 "succs", "prio", "start", "finish", "semn", "semv", "bonus")

    def __init__(self, oid, engine, fns, dma, cost, act_set):
        self.oid = oid
        self.engine = engine
        self.fns = fns
        self.dma = dma
        self.cost = cost
        self.act_set = act_set
        self.preds = set()
        self.succs = []
        self.prio = 0.0
        self.start = 0.0
        self.finish = 0.0
        self.semn = None
        self.semv = 0
        self.bonus = 0.0


ACT_SWITCH_NS = 1400.0


class P:

    def __init__(self):
        self.ops = []
        self.lastw = {}
        self.readers = {}
        self.default_bonus = 0.0

    def op(self, engine, fns, reads=(), writes=(), dma=False, cost=150.0,
           act_set=None, prio_bonus=0.0):
        if callable(fns):
            fns = [fns]
        o = Op(len(self.ops), engine, fns, dma, float(cost), act_set)
        o.bonus = float(prio_bonus) if prio_bonus else self.default_bonus
        for t in reads:
            w = self.lastw.get(t)
            if w is not None:
                o.preds.add(w)
        for t in writes:
            w = self.lastw.get(t)
            if w is not None:
                o.preds.add(w)
            for r in self.readers.get(t, ()):
                o.preds.add(r)
        o.preds.discard(o.oid)
        for t in writes:
            self.lastw[t] = o.oid
            self.readers[t] = []
        for t in reads:
            self.readers.setdefault(t, []).append(o.oid)
        self.ops.append(o)
        return o.oid

    def schedule(self):
        import heapq
        ops = self.ops
        for o in ops:
            for pid in o.preds:
                ops[pid].succs.append(o.oid)

        def blevel(scale):
            prio = [0.0] * len(ops)
            for o in reversed(ops):
                m = 0.0
                for s in o.succs:
                    if prio[s] > m:
                        m = prio[s]
                prio[o.oid] = o.cost * scale.get(o.engine, 1.0) + o.bonus + m
            return prio

        def greedy(prio, simscale):
            indeg = [len(o.preds) for o in ops]
            ready = {e: [] for e in ENGINES}
            data_ready = [0.0] * len(ops)
            finish = [0.0] * len(ops)
            for o in ops:
                if indeg[o.oid] == 0:
                    heapq.heappush(ready[o.engine], (-prio[o.oid], o.oid))
            engine_free = {e: 0.0 for e in ENGINES}
            queue_free = {"sync": 0.0, "scalar": 0.0}
            act_cur = None
            sched = {e: [] for e in ENGINES}
            n_left = len(ops)
            while n_left:
                best = None
                for e in ENGINES:
                    if not ready[e]:
                        continue
                    nprio, oid = ready[e][0]
                    o = ops[oid]
                    st = max(engine_free[e], data_ready[oid])
                    if o.dma:
                        st = max(st, queue_free[e])
                    if e == "scalar" and o.act_set and act_cur and \
                            o.act_set != act_cur:
                        st += ACT_SWITCH_NS
                    if best is None or st < best[0] or \
                            (st == best[0] and nprio < best[1]):
                        best = (st, nprio, e, oid)
                st, _, e, oid = best
                heapq.heappop(ready[e])
                o = ops[oid]
                finish[oid] = st + o.cost * simscale.get(e, 1.0)
                engine_free[e] = finish[oid] if not o.dma else st + 100.0
                if o.dma:
                    queue_free[e] = finish[oid]
                if e == "scalar" and o.act_set:
                    act_cur = o.act_set
                sched[e].append(oid)
                n_left -= 1
                for s in o.succs:
                    if finish[oid] > data_ready[s]:
                        data_ready[s] = finish[oid]
                    indeg[s] -= 1
                    if indeg[s] == 0:
                        heapq.heappush(ready[ops[s].engine], (-prio[s], s))
            return max(finish), sched

        est, sched = greedy(blevel({"vector": 2.5, "scalar": 1.35}),
                            {"vector": 1.25})
        self.sched = sched
        return est

    def emit(self, nc, sems):
        est = self.schedule()
        ops = self.ops
        cnt = {s: 0 for s in ALLSEMS}
        qidx = {"sync": 0, "scalar": 0}
        for e in ENGINES:
            for oid in self.sched[e]:
                o = ops[oid]
                if o.dma:
                    pool = QPOOL[e]
                    o.semn = pool[qidx[e] % len(pool)]
                    qidx[e] += 1
                    cnt[o.semn] += 16
                else:
                    o.semn = ESEM[e]
                    cnt[o.semn] += 1
                o.semv = cnt[o.semn]
        totals = dict(cnt)
        with nc.Block() as block:
            for ename in ENGINES:
                def section(eng, ename=ename):
                    clk = {}
                    for oid in self.sched[ename]:
                        o = ops[oid]
                        waits = {}
                        for pid in o.preds:
                            p_ = ops[pid]
                            if clk.get(p_.semn, 0) < p_.semv:
                                if waits.get(p_.semn, 0) < p_.semv:
                                    waits[p_.semn] = p_.semv
                        if o.dma and o.semv > 16:
                            prev = o.semv - 16
                            if clk.get(o.semn, 0) < prev and \
                                    waits.get(o.semn, 0) < prev:
                                waits[o.semn] = prev
                        for s, v in sorted(waits.items()):
                            eng.wait_ge(sems[s], v)
                            if clk.get(s, 0) < v:
                                clk[s] = v
                        inst = None
                        for fn in o.fns:
                            inst = fn(eng)
                        if inst is not None:
                            inst.then_inc(sems[o.semn], 16 if o.dma else 1)
                    if ename == "gpsimd":
                        for s in QPOOL["sync"] + QPOOL["scalar"]:
                            if totals.get(s, 0) > 0:
                                eng.wait_ge(sems[s], totals[s])
                getattr(block, ename)(section)
        return est


def build_program():
    nc = bass.Bass("TRN2", target_bir_lowering=False)

    xd = nc.dram_tensor("x", [T, D], F32, kind="ExternalInput").ap()
    wtd = nc.dram_tensor("wt", [D, D], F32, kind="ExternalInput").ap()
    utd = nc.dram_tensor("ut", [D, 128], F32, kind="ExternalInput").ap()
    vpd = nc.dram_tensor("vp", [128, D], F32, kind="ExternalInput").ap()
    gwd = nc.dram_tensor("gw", [8, D], F32, kind="ExternalInput").ap()
    emad = nc.dram_tensor("ema", [1, D], F32, kind="ExternalInput").ap()
    scald = nc.dram_tensor("scal", [1, 4], F32, kind="ExternalInput").ap()
    mmd = nc.dram_tensor("maskm", [128, 128], BF16, kind="ExternalInput").ap()
    mod = nc.dram_tensor("masko", [128, 128], BF16, kind="ExternalInput").ap()
    wmd = nc.dram_tensor("wm16", [128, NSLOT], F32, kind="ExternalInput").ap()
    idd = nc.dram_tensor("ident", [128, 128], BF16, kind="ExternalInput").ap()
    yd = nc.dram_tensor("y", [NSLOT * 128, D], F32, kind="ExternalOutput").ap()

    ex = contextlib.ExitStack()
    sb = lambda name, shape, dt: ex.enter_context(nc.sbuf_tensor(name, shape, dt))

    kT = sb("kT", [128, NCHUNK, T], BF16)
    h = sb("h", [128, NSLOT, D], F32)
    xbuf = sb("xbuf", [128, 10, D], F32)
    ztile = sb("ztile", [128, 1, D], F32)
    ktok = sb("ktok", [128, 2, D], BF16)
    z2tok = sb("z2tok", [128, 2, D], BF16)
    wTb = sb("wTb", [128, NCHUNK, D], BF16)
    uTb = sb("uTb", [128, NCHUNK, 128], BF16)
    gwTb = sb("gwTb", [128, NCHUNK, 8], BF16)
    vPb = sb("vPb", [128, D], BF16)

    ident = sb("identb", [128, 128], BF16)
    maskm = sb("maskmb", [128, 128], BF16)
    masko = sb("maskob", [128, 128], BF16)
    wm16 = sb("wm16b", [128, NSLOT], F32)
    emaf = sb("emaf", [128, NCHUNK], BF16)
    srow = sb("srow", [1, 8], F32)
    crow = sb("crow", [1, 8], F32)
    onesrow = sb("onesrow", [1, 128], F32)
    constcols = sb("constcols", [128, 8], F32)
    tiny = sb("tiny", [1, 8], F32)
    st6 = sb("st6", [128, 2, GS, 12], F32)
    mvg = sb("mvg", [128, 2, GS, 2], F32)
    rstd8 = sb("rstd8", [128, 2, GS], F32)
    rtmp8 = sb("rtmp8", [128, 2, GS], F32)
    ss8 = sb("ss8", [128, 2, GS], F32)
    inv8 = sb("inv8", [128, 2, GS], F32)
    mb8 = sb("mb8", [128, 2, GS], F32)
    colbuf = sb("colbuf", [128, 2, 16], F32)
    sraw16 = sb("sraw16", [128, NSLOT], F32)
    ecol16 = sb("ecol16", [128, NSLOT], F32)
    gtmp16 = sb("gtmp16", [128, NSLOT], F32)
    gate16 = sb("gate16", [128, NSLOT], F32)
    mv2 = sb("mv2", [128, NSLOT, 2], F32)
    st26 = sb("st26", [128, 2, 12], F32)
    rstd16 = sb("rstd16", [128, NSLOT], F32)
    rtmp16 = sb("rtmp16", [128, NSLOT], F32)
    nmax2 = sb("nmax2", [128, 2], F32)
    ssum2 = sb("ssum2", [128, 2], F32)
    rinv2 = sb("rinv2", [128, 2], F32)
    gexp2 = sb("gexp2", [128, 2, 8], F32)
    tgb = sb("tgb", [128, 2, 128], BF16)
    tgTb = sb("tgTb", [128, 2, 128], BF16)
    constf = sb("constf", [128, 2], F32)
    identf = sb("identf", [128, 128], F32)

    pst = [ex.enter_context(nc.psum_tensor(f"ps{i}", [128, 512], F32))
           for i in range(8)]
    pspools = {"big": [0, 1, 2, 3, 4], "tr": [6, 7], "one": [5]}
    psidx = {"big": 0, "tr": 0, "one": 0}

    def psalloc(pool="big"):
        lst = pspools[pool]
        i = lst[psidx[pool] % len(lst)]
        psidx[pool] += 1
        return pst[i], ("ps", i)

    sems = {s: ex.enter_context(nc.semaphore(s)) for s in ALLSEMS}

    wstage = xbuf
    emarow = ztile[0:1, 0, :]
    escr = xbuf[0:1, 8, :]
    embf = ktok[0:1, 0, :]
    gwbf = ktok[0:8, 1, :]

    p = P()

    p.op("gpsimd", lambda e: e.memset(constf[:, 0:1], EPS), writes=[("constf",)])
    p.op("gpsimd", lambda e: e.memset(constf[:, 1:2], 1e-24), writes=[("constf",)])
    p.op("scalar", lambda e: e.dma_start(maskm[:, :], mmd), writes=[("maskm",)], dma=True)
    p.op("scalar", lambda e: e.dma_start(masko[:, :], mod), writes=[("masko",)], dma=True)
    p.op("scalar", lambda e: e.dma_start(wm16[:, :], wmd), writes=[("wm16",)], dma=True)
    p.op("scalar", lambda e: e.dma_start(ident[:, :], idd), writes=[("ident",)], dma=True)
    p.op("vector", lambda e: e.tensor_copy(identf[:, :], ident[:, :]),
         reads=[("ident",)], writes=[("identf",)], cost=200.0)
    p.op("scalar", lambda e: e.dma_start(srow[:, 0:4], scald), writes=[("srow",)], dma=True)
    p.op("scalar", lambda e: e.dma_start(emarow, emad), writes=[("zt", 0)], dma=True)
    p.op("scalar", lambda e: e.dma_start(wstage[0:8, 9, :], gwd),
         writes=[("xbuf", 9)], dma=True)

    p.default_bonus = 60000.0
    p.op("scalar", lambda e: e.activation(crow[:, 4:5], srow[:, 0:1], AF.Exp),
         reads=[("srow",)], writes=[("crow",)])
    p.op("scalar", lambda e: e.activation(crow[:, 5:6], srow[:, 1:2], AF.Exp, scale=-1.0),
         reads=[("srow",)], writes=[("crow",)])
    p.op("scalar", lambda e: e.activation(crow[:, 6:7], srow[:, 2:3], AF.Exp, scale=-1.0),
         reads=[("srow",)], writes=[("crow",)])
    p.op("vector", lambda e: e.tensor_scalar(crow[:, 5:6], crow[:, 5:6], 1.0, None, ALU.add),
         reads=[("crow",)], writes=[("crow",)])
    p.op("vector", lambda e: e.reciprocal(crow[:, 2:3], crow[:, 5:6]),
         reads=[("crow",)], writes=[("crow",)])
    p.op("vector", lambda e: e.tensor_scalar(crow[:, 3:4], crow[:, 2:3], -1.0, 1.0, ALU.mult, ALU.add),
         reads=[("crow",)], writes=[("crow",)])
    p.op("vector", lambda e: e.tensor_scalar(crow[:, 6:7], crow[:, 6:7], 1.0, None, ALU.add),
         reads=[("crow",)], writes=[("crow",)])
    p.op("vector", lambda e: e.reciprocal(crow[:, 7:8], crow[:, 6:7]),
         reads=[("crow",)], writes=[("crow",)])
    p.op("vector", lambda e: e.tensor_tensor(crow[:, 0:1], crow[:, 4:5], crow[:, 7:8], ALU.mult),
         reads=[("crow",)], writes=[("crow",)])
    p.op("vector", lambda e: e.tensor_scalar(crow[:, 6:7], crow[:, 7:8], -1.0, 1.0, ALU.mult, ALU.add),
         reads=[("crow",)], writes=[("crow",)])
    p.op("vector", lambda e: e.tensor_tensor(crow[:, 1:2], crow[:, 4:5], crow[:, 6:7], ALU.mult),
         reads=[("crow",)], writes=[("crow",)])
    p.op("vector", lambda e: e.memset(onesrow[:, :], 1.0), writes=[("ones",)])
    psC, tC = psalloc("big")
    p.op("tensor", lambda e: e.matmul(psC[:, 0:8], onesrow[:, :], crow[:, :],
                                      start=True, stop=True),
         reads=[("ones",), ("crow",)], writes=[tC])
    p.op("vector", lambda e: e.tensor_copy(constcols[:, :], psC[:, 0:8]),
         reads=[tC], writes=[("constcols",)])
    A_col = constcols[:, 0:1]
    B_col = constcols[:, 1:2]
    al_col = constcols[:, 2:3]
    be_col = constcols[:, 3:4]

    p.op("scalar", lambda e: e.activation(escr, emarow, AF.Square,
                                          accum_out=tiny[:, 0:1]),
         reads=[("zt", 0)], writes=[("xbuf", 8), ("tiny",)])
    p.op("vector", lambda e: e.tensor_scalar(tiny[:, 0:1], tiny[:, 0:1], 1e-24, None, ALU.max),
         reads=[("tiny",)], writes=[("tiny",)])
    p.op("scalar", lambda e: e.activation(tiny[:, 1:2], tiny[:, 0:1], AF.Ln),
         reads=[("tiny",)], writes=[("tiny",)])
    p.op("scalar", lambda e: e.activation(tiny[:, 2:3], tiny[:, 1:2], AF.Exp, scale=-0.5),
         reads=[("tiny",)], writes=[("tiny",)])
    p.op("vector", lambda e: e.tensor_scalar(embf, emarow, tiny[:, 2:3], None, ALU.mult),
         reads=[("zt", 0), ("tiny",)], writes=[("ktok", 0)])
    psE, tE = psalloc("big")
    fns = []
    for c in range(NCHUNK):
        fns.append(lambda e, c=c: e.matmul(psE[:, c:c + 1], embf[:, c * 128:(c + 1) * 128],
                                           ident[0:1, 0:1], start=(c == 0), stop=(c == 7),
                                           skip_group_check=True))
    p.op("tensor", fns, reads=[("ktok", 0), ("ident",)], writes=[tE])
    p.op("vector", lambda e: e.tensor_copy(emaf[:, :], psE[:, 0:8]),
         reads=[tE], writes=[("emaf",)])

    p.op("vector", lambda e: e.tensor_copy(gwbf, wstage[0:8, 9, :]),
         reads=[("xbuf", 9)], writes=[("ktok", 1)])
    psG, tG = psalloc("big")
    fns = []
    for c in range(NCHUNK):
        fns.append(lambda e, c=c: e.matmul(psG[:, c * 8:(c + 1) * 8],
                                           gwbf[:, c * 128:(c + 1) * 128],
                                           ident[0:8, 0:8], start=(c == 0), stop=(c == 7),
                                           skip_group_check=True))
    p.op("tensor", fns, reads=[("ktok", 1), ("ident",)], writes=[tG])
    p.op("vector", lambda e: e.tensor_copy(
        gwTb[:, :, :], psG[:, 0:64].rearrange("p (c e) -> p c e", c=8)),
        reads=[tG], writes=[("gwTb",)])

    p.default_bonus = 0.0
    for c in range(NCHUNK):
        p.op("scalar", lambda e, c=c: e.dma_start(
            wstage[:, 8 + c % 2, :], wtd[c * 128:(c + 1) * 128, :]),
            writes=[("xbuf", 8 + c % 2)], dma=True, cost=2500.0)
        p.op("vector", lambda e, c=c: e.tensor_copy(wTb[:, c, :], wstage[:, 8 + c % 2, :]),
             reads=[("xbuf", 8 + c % 2)], writes=[("wTb",)], cost=600.0)
    p.op("scalar", lambda e: e.dma_start(
        wstage[:, 8, :].rearrange("p (c j) -> p c j", c=8),
        utd.rearrange("(c q) j -> q c j", q=128)),
        writes=[("xbuf", 8)], dma=True)
    p.op("vector", lambda e: e.tensor_copy(
        uTb[:, :, :], wstage[:, 8, :].rearrange("p (c j) -> p c j", c=8)),
        reads=[("xbuf", 8)], writes=[("uTb",)])
    p.op("scalar", lambda e: e.dma_start(wstage[:, 9, :], vpd),
         writes=[("xbuf", 9)], dma=True)
    p.op("vector", lambda e: e.tensor_scalar(vPb[:, :], wstage[:, 9, :], 0.0625, None, ALU.mult),
         reads=[("xbuf", 9)], writes=[("vPb",)])

    order = []
    for j in range(NSLOT):
        order.append(j)
        order.append(16 + j)

    def kcol(tid):
        return 2048 * (tid // 16) + 128 * (tid % 16)

    xview = xd.rearrange("(g t q) d -> q g t d", g=2, q=128)

    def emit_key_tile(gi, i, tid, ri):
        g2 = gi % 2
        if ri % 2 == 0:
            j = tid
            p.op("sync", lambda e, ri=ri, j=j: e.dma_start(
                xbuf[:, ri:ri + 2, :], xview[:, :, j, :]),
                writes=[("xbuf", ri), ("xbuf", ri + 1)], dma=True, cost=3700.0)
        p.op("vector", [
            lambda e: e.bn_stats(st6[:, g2, i, 0:6], xbuf[:, ri, 0:512]),
            lambda e: e.bn_stats(st6[:, g2, i, 6:12], xbuf[:, ri, 512:1024]),
        ], reads=[("xbuf", ri)], writes=[("st6", g2, i)], cost=1250.0)
        p.op("vector", lambda e: e.bn_aggr(mvg[:, g2, i, :], st6[:, g2, i, :]),
             reads=[("st6", g2, i)], writes=[("mv", g2, i)], cost=90.0)

    def emit_key_tile_b(gi, i, tid, ri):
        g2 = gi % 2
        buf = ri % 2
        mine = tid < 16
        gdst = h[:, tid, :] if mine else xbuf[:, ri, :]
        gtok = ("h", tid) if mine else ("xbuf", ri)
        p.op("scalar", lambda e: e.activation(
            gdst, xbuf[:, ri, :], AF.Gelu_apprx_tanh,
            bias=mb8[:, g2, i:i + 1], scale=rstd8[:, g2, i:i + 1]),
            reads=[("xbuf", ri), ("rstd8", g2), ("mb8", g2)], writes=[gtok],
            cost=1150.0, act_set="gelu")
        scr = xbuf[:, ri, :] if mine else ztile[:, 0, :]
        stok = ("xbuf", ri) if mine else ("zt", 0)
        if tid % 2 == 0:
            p.op("scalar", lambda e: e.activation(
                scr, gdst, AF.Square, accum_out=ss8[:, g2, i:i + 1]),
                reads=[gtok], writes=[stok, ("ss", g2, i)], cost=1150.0)
        else:
            p.op("vector", lambda e: e.scalar_tensor_tensor(
                scr, gdst, 1.0, gdst, ALU.bypass, ALU.mult,
                accum_out=ss8[:, g2, i:i + 1]),
                reads=[gtok], writes=[stok, ("ss", g2, i)], cost=1230.0)

    def emit_key_tile_c(gi, i, tid, ri):
        g2 = gi % 2
        buf = ri % 2
        mine = tid < 16
        gdst = h[:, tid, :] if mine else xbuf[:, ri, :]
        gtok = ("h", tid) if mine else ("xbuf", ri)
        col = kcol(tid)
        p.op("gpsimd", lambda e: e.tensor_tensor(
            ktok[:, buf, :], gdst,
            inv8[:, g2, i:i + 1].broadcast_to([128, D]), ALU.mult),
            reads=[gtok, ("inv8", g2)], writes=[("ktok", buf)], cost=2300.0)
        pa, ta = psalloc("tr")
        pb, tb = psalloc("tr")
        fns = []
        for c in range(NCHUNK):
            ps = pa if c < 4 else pb
            fns.append(lambda e, c=c, ps=ps: e.matmul(
                ps[:, (c % 4) * 128:(c % 4) * 128 + 128],
                ktok[:, buf, c * 128:(c + 1) * 128], ident[:, :],
                start=(c % 4 == 0), stop=(c % 4 == 3), skip_group_check=True))
        p.op("tensor", fns, reads=[("ktok", buf), ("ident",)], writes=[ta, tb],
             cost=900.0)
        p.op("vector", lambda e: e.tensor_copy(
            kT[:, 0:4, col:col + 128],
            pa[:, :].rearrange("p (c f) -> p c f", c=4)),
            reads=[ta], writes=[("kT", tid, 0)], cost=660.0)
        p.op("scalar", lambda e: e.copy(
            kT[:, 4:8, col:col + 128],
            pb[:, :].rearrange("p (c f) -> p c f", c=4)),
            reads=[tb], writes=[("kT", tid, 1)], cost=580.0)

    def qk_slot(m):
        L = m + 1
        q, rem = divmod(L, 4)
        blocks = [[512 * i, 512, False] for i in range(q)]
        if rem:
            blocks.append([512 * q, 128 * rem, True])
        else:
            blocks[-1][2] = True
        allb = [("m", bs, W, iswin) for (bs, W, iswin) in blocks] + \
               [("o", bs, W, iswin) for (bs, W, iswin) in blocks]
        cbuf = m % 2
        cb_idx = 0
        groups = [allb[i:i + 2] for i in range(0, len(allb), 2)]
        qtok = [("kT", m, 0), ("kT", m, 1)]
        for gi2, grp in enumerate(groups):
            ptoks = []
            pts = []
            reads = list(qtok) + [("ident",)]
            for (side, bs, W, iswin) in grp:
                pt, tk = psalloc()
                pts.append(pt)
                ptoks.append(tk)
                base = 0 if side == "m" else 16
                for j in range(bs // 128, (bs + W - 1) // 128 + 1):
                    reads.append(("kT", base + j, 0))
                    reads.append(("kT", base + j, 1))
                if iswin:
                    reads.append(("maskm",) if side == "m" else ("masko",))
            has_ema = (gi2 == 0)
            if has_ema:
                pe_t, tk_e = psalloc("one")
                ptoks.append(tk_e)
                reads.append(("emaf",))
            fns = []
            for c in range(NCHUNK):
                for bi, (side, bs, W, iswin) in enumerate(grp):
                    off = (0 if side == "m" else 2048) + bs
                    fns.append(lambda e, c=c, pt=pts[bi], off=off, W=W, iswin=iswin: e.matmul(
                        pt[:, 0:W],
                        kT[:, c, 128 * m:128 * m + 128],
                        kT[:, c, off:off + W],
                        start=(c == 0), stop=(c == 7 and not iswin),
                        skip_group_check=True))
                if has_ema:
                    fns.append(lambda e, c=c, pe_t=pe_t: e.matmul(
                        pe_t[:, 0:1], kT[:, c, 128 * m:128 * m + 128],
                        emaf[:, c:c + 1],
                        start=(c == 0), stop=(c == 7), skip_group_check=True))
            for bi, (side, bs, W, iswin) in enumerate(grp):
                if iswin:
                    msk = maskm if side == "m" else masko
                    fns.append(lambda e, pt=pts[bi], W=W, msk=msk: e.matmul(
                        pt[:, W - 128:W], ident[:, :], msk[:, :],
                        start=False, stop=True, skip_group_check=True))
            mmcost = (8 * sum(W for (_, _, W, _) in grp)
                      + 128 * sum(1 for b in grp if b[3])) / 2.4 + 8 * 110.0
            p.op("tensor", fns, reads=reads, writes=ptoks, cost=mmcost)
            for bi, (side, bs, W, iswin) in enumerate(grp):
                p.op("vector", lambda e, pt=pts[bi], W=W, ci=cb_idx: e.reduce_max(
                    colbuf[:, cbuf, ci:ci + 1], pt[:, 0:W], axis=AX),
                    reads=[ptoks[bi]], writes=[("colbuf", cbuf)],
                    cost=(130 + W) / 0.96)
                cb_idx += 1
            if has_ema:
                p.op("vector", lambda e, pe_t=pe_t: e.tensor_copy(ecol16[:, m:m + 1], pe_t[:, 0:1]),
                     reads=[tk_e], writes=[("ecol", m)])
        p.op("vector", lambda e, n=cb_idx: e.reduce_max(
            sraw16[:, m:m + 1], colbuf[:, cbuf, 0:n], axis=AX),
            reads=[("colbuf", cbuf)], writes=[("sraw", m)])

    def keygroup(gi, t0, n):
        g2 = gi % 2
        tiles = order[t0: t0 + n]
        for i, tid in enumerate(tiles):
            emit_key_tile(gi, i, tid, (GS * gi + i) % 10)
        p.op("scalar", lambda e, g2=g2: e.activation(
            rtmp8[:, g2, :].unsqueeze(2), mvg[:, g2, :, 1:2], AF.Ln, bias=constf[:, 0:1]),
            reads=[("mv", g2, i) for i in range(GS)] + [("constf",)],
            writes=[("rtmp8", g2)], cost=310.0, act_set="lnexp")
        p.op("scalar", lambda e, g2=g2: e.activation(
            rstd8[:, g2, :], rtmp8[:, g2, :], AF.Exp, scale=-0.5),
            reads=[("rtmp8", g2)], writes=[("rstd8", g2)], cost=310.0, act_set="lnexp")
        p.op("vector", lambda e, g2=g2: e.scalar_tensor_tensor(
            mb8[:, g2, :].unsqueeze(2), mvg[:, g2, :, 0:1], -1.0,
            rstd8[:, g2, :].unsqueeze(2), ALU.mult, ALU.mult),
            reads=[("rstd8", g2)] + [("mv", g2, i) for i in range(GS)],
            writes=[("mb8", g2)], cost=160.0)
        for i, tid in enumerate(tiles):
            emit_key_tile_b(gi, i, tid, (GS * gi + i) % 10)
        p.op("vector", lambda e, g2=g2: e.tensor_scalar(
            ss8[:, g2, :], ss8[:, g2, :], 1e-24, None, ALU.max),
            reads=[("ss", g2, i) for i in range(GS)],
            writes=[("ss", g2, i) for i in range(GS)])
        p.op("scalar", lambda e, g2=g2: e.activation(
            rtmp8[:, g2, :], ss8[:, g2, :], AF.Ln),
            reads=[("ss", g2, i) for i in range(GS)], writes=[("rtmp8", g2)],
            cost=310.0, act_set="lnexp")
        p.op("scalar", lambda e, g2=g2: e.activation(
            inv8[:, g2, :], rtmp8[:, g2, :], AF.Exp, scale=-0.5),
            reads=[("rtmp8", g2)], writes=[("inv8", g2)], cost=310.0, act_set="lnexp")
        for j in range(n // 2):
            emit_key_tile_c(gi, 2 * j, tiles[2 * j], (t0 + 2 * j) % 10)
            emit_key_tile_c(gi, 2 * j + 1, tiles[2 * j + 1], (t0 + 2 * j + 1) % 10)
            qk_slot((t0 + 2 * j) // 2)


    def gates_batch(g4):
        lo, hi = 4 * g4, 4 * g4 + 4
        srd = [("sraw", m) for m in range(lo, hi)] + [("ecol", m) for m in range(lo, hi)]
        gt = gtmp16[:, lo:hi]
        ec = ecol16[:, lo:hi]
        if g4 == 0:
            p.op("vector", lambda e: e.tensor_tensor(gt, sraw16[:, lo:hi], ec, ALU.subtract),
                 reads=srd, writes=[("gtmp", g4)])
            p.op("vector", lambda e: e.tensor_tensor(gt, gt, wm16[:, lo:hi], ALU.mult),
                 reads=[("gtmp", g4), ("wm16",)], writes=[("gtmp", g4)])
            p.op("vector", lambda e: e.tensor_tensor(gt, gt, ec, ALU.add),
                 reads=[("gtmp", g4)] + [("ecol", m) for m in range(lo, hi)],
                 writes=[("gtmp", g4)])
            p.op("vector", lambda e: e.tensor_scalar(gt, gt, A_col, None, ALU.mult),
                 reads=[("gtmp", g4), ("constcols",)], writes=[("gtmp", g4)])
        else:
            p.op("vector", lambda e: e.tensor_scalar(gt, sraw16[:, lo:hi], A_col, None, ALU.mult),
                 reads=srd + [("constcols",)], writes=[("gtmp", g4)])
        p.op("vector", lambda e: e.scalar_tensor_tensor(
            gt, ec, B_col, gt, ALU.mult, ALU.add),
            reads=[("gtmp", g4), ("constcols",)] + [("ecol", m) for m in range(lo, hi)],
            writes=[("gtmp", g4)])
        p.op("scalar", lambda e: e.activation(gate16[:, lo:hi], gt, AF.Exp, scale=-1.0),
             reads=[("gtmp", g4)], writes=[("gates", g4)], cost=310.0, act_set="lnexp")
        p.op("vector", lambda e: e.tensor_scalar(gate16[:, lo:hi], gate16[:, lo:hi],
                                                 al_col, be_col, ALU.mult, ALU.add),
             reads=[("gates", g4), ("constcols",)], writes=[("gates", g4)])

    def g_slot(m):
        i = m % 10
        buf = m % 2
        if m % 2 == 0:
            p.op("scalar", lambda e, m=m, i=i: e.dma_start(
                xbuf[:, i:i + 2, :],
                xd[m * 128:(m + 2) * 128, :].rearrange("(t q) d -> q t d", q=128)),
                writes=[("xbuf", i), ("xbuf", i + 1)], dma=True, cost=3700.0)
        p.op("vector", lambda e, m=m, i=i: e.scalar_tensor_tensor(
            h[:, m, :], h[:, m, :], gate16[:, m:m + 1], xbuf[:, i, :],
            ALU.mult, ALU.add),
            reads=[("h", m), ("gates", m // 4), ("xbuf", i)], writes=[("h", m)],
            cost=1230.0)
        p.op("vector", [
            lambda e, m=m: e.bn_stats(st26[:, m % 2, 0:6], h[:, m, 0:512]),
            lambda e, m=m: e.bn_stats(st26[:, m % 2, 6:12], h[:, m, 512:1024]),
        ], reads=[("h", m)], writes=[("st26", m % 2)], cost=1250.0)
        p.op("vector", lambda e, m=m: e.bn_aggr(mv2[:, m, :], st26[:, m % 2, :]),
             reads=[("st26", m % 2)], writes=[("mv2", m)], cost=90.0)
        p.op("scalar", lambda e, m=m: e.activation(
            rtmp16[:, m:m + 1], mv2[:, m, 1:2], AF.Ln, bias=constf[:, 0:1]),
            reads=[("mv2", m), ("constf",)], writes=[("rtmp16", m)],
            cost=300.0, act_set="lnexp")
        p.op("scalar", lambda e, m=m: e.activation(
            rstd16[:, m:m + 1], rtmp16[:, m:m + 1], AF.Exp, scale=-0.5),
            reads=[("rtmp16", m)], writes=[("rstd16", m)], cost=300.0, act_set="lnexp")

    t0 = 0
    for gi, n in enumerate([GS] * (NTILE // GS)):
        keygroup(gi, t0, n)
        t0 += n

    for g4 in range(4):
        gates_batch(g4)
        for m in range(4 * g4, 4 * g4 + 4):
            g_slot(m)

    def ffn_part1(m):
        buf = m % 2
        p.op("vector", lambda e: e.scalar_tensor_tensor(
            rtmp16[:, m:m + 1], mv2[:, m, 0:1], -1.0, rstd16[:, m:m + 1],
            ALU.mult, ALU.mult),
            reads=[("mv2", m), ("rstd16", m)], writes=[("rtmp16", m)], cost=90.0)
        p.op("scalar", lambda e: e.activation(
            z2tok[:, buf, :], h[:, m, :], AF.Identity,
            bias=rtmp16[:, m:m + 1], scale=rstd16[:, m:m + 1]),
            reads=[("h", m), ("rtmp16", m), ("rstd16", m)], writes=[("z2", buf)],
            cost=700.0, act_set="lnexp")
        pa, ta = psalloc("tr")
        pb, tb = psalloc("tr")
        fns = []
        for c in range(NCHUNK):
            ps = pa if c < 4 else pb
            fns.append(lambda e, c=c, ps=ps: e.matmul(
                ps[:, (c % 4) * 128:(c % 4) * 128 + 128],
                z2tok[:, buf, c * 128:(c + 1) * 128], ident[:, :],
                start=(c % 4 == 0), stop=(c % 4 == 3), skip_group_check=True))
        p.op("tensor", fns, reads=[("z2", buf), ("ident",)], writes=[ta, tb],
             cost=900.0)
        col = 128 * m
        p.op("vector", lambda e: e.tensor_copy(
            kT[:, 0:4, col:col + 128], pa[:, :].rearrange("p (c f) -> p c f", c=4)),
            reads=[ta], writes=[("kT", m, 0)], cost=660.0)
        p.op("scalar", lambda e: e.copy(
            kT[:, 4:8, col:col + 128], pb[:, :].rearrange("p (c f) -> p c f", c=4)),
            reads=[tb], writes=[("kT", m, 1)], cost=580.0)
        z2toks = [("kT", m, 0), ("kT", m, 1)]
        psA, tA = psalloc("big")
        psB, tB = psalloc("big")
        psT, tT = psalloc("one")
        fnsT = []
        fnsAB = []
        for c in range(NCHUNK):
            lhs = kT[:, c, 128 * m:128 * m + 128]
            fnsT.append(lambda e, c=c, lhs=lhs, psT=psT: e.matmul(
                psT[:, 0:128], lhs, uTb[:, c, :],
                start=(c == 0), stop=(c == 7), skip_group_check=True))
            fnsT.append(lambda e, c=c, lhs=lhs, psT=psT: e.matmul(
                psT[:, 128:136], lhs, gwTb[:, c, :],
                start=False, stop=(c == 7), skip_group_check=True))
            fnsAB.append(lambda e, c=c, lhs=lhs, psA=psA: e.matmul(
                psA[:, :], lhs, wTb[:, c, 0:512],
                start=(c == 0), stop=False, skip_group_check=True))
            fnsAB.append(lambda e, c=c, lhs=lhs, psB=psB: e.matmul(
                psB[:, :], lhs, wTb[:, c, 512:1024],
                start=(c == 0), stop=False, skip_group_check=True))
        p.op("tensor", fnsT, reads=z2toks + [("uTb",), ("gwTb",)],
             writes=[tT], cost=1350.0)
        p.op("tensor", fnsAB, reads=z2toks + [("wTb",)],
             writes=[tA, tB], cost=3550.0)
        p.op("vector", lambda e, psT=psT: e.reduce_max(
            nmax2[:, buf:buf + 1], psT[:, 128:136], axis=AX, negate=True),
            reads=[tT], writes=[("nmax", buf)])
        p.op("scalar", lambda e, psT=psT: e.activation(
            gexp2[:, buf, :], psT[:, 128:136], AF.Exp, bias=nmax2[:, buf:buf + 1]),
            reads=[tT, ("nmax", buf)], writes=[("gexp", buf)],
            cost=310.0, act_set="lnexp")
        p.op("vector", lambda e: e.reduce_sum(
            ssum2[:, buf:buf + 1], gexp2[:, buf, :], axis=AX),
            reads=[("gexp", buf)], writes=[("ssum", buf)])
        p.op("vector", lambda e: e.reciprocal(
            rinv2[:, buf:buf + 1], ssum2[:, buf:buf + 1]),
            reads=[("ssum", buf)], writes=[("rinv", buf)])
        p.op("vector", lambda e: e.tensor_scalar(
            gexp2[:, buf, :], gexp2[:, buf, :], rinv2[:, buf:buf + 1], None, ALU.mult),
            reads=[("gexp", buf), ("rinv", buf)], writes=[("gexp", buf)])
        p.op("vector", lambda e, psT=psT: e.tensor_tensor(
            tgb[:, buf, :].rearrange("p (ee r) -> p ee r", ee=8),
            psT[:, 0:128].rearrange("p (ee r) -> p ee r", ee=8),
            gexp2[:, buf, :].unsqueeze(2).broadcast_to([128, 8, 16]),
            ALU.mult),
            reads=[tT, ("gexp", buf)], writes=[("tg", buf)], cost=280.0)
        return (m, buf, psA, tA, psB, tB, psT, tT)

    def ffn_part2(st):
        m, buf, psA, tA, psB, tB, psT, tT = st
        psT2, tT2 = psalloc("tr")
        p.op("tensor", lambda e: e.matmul(
            psT2[:, 0:128], tgb[:, buf, :], ident[:, :],
            start=True, stop=True, skip_group_check=True),
            reads=[("tg", buf), ("ident",)], writes=[tT2])
        p.op("scalar", lambda e: e.activation(
            tgTb[:, buf, :], psT2[:, 0:128], AF.Identity),
            reads=[tT2], writes=[("tgT", buf)], cost=280.0, act_set="lnexp")
        p.op("tensor", [
            lambda e: e.matmul(psA[:, :], tgTb[:, buf, :], vPb[:, 0:512],
                               start=False, stop=True, skip_group_check=True),
            lambda e: e.matmul(psB[:, :], tgTb[:, buf, :], vPb[:, 512:1024],
                               start=False, stop=True, skip_group_check=True),
        ], reads=[("tgT", buf), ("vPb",)], writes=[tA, tB], cost=560.0)
        p.op("vector", lambda e: e.scalar_tensor_tensor(
            h[:, m, 0:512], psA[:, :], 1.0, h[:, m, 0:512], ALU.bypass, ALU.add),
            reads=[tA, ("h", m)], writes=[("h", m)], cost=660.0)
        p.op("vector", lambda e: e.scalar_tensor_tensor(
            h[:, m, 512:1024], psB[:, :], 1.0, h[:, m, 512:1024], ALU.bypass, ALU.add),
            reads=[tB, ("h", m)], writes=[("h", m)], cost=660.0)
        if m % 2 == 1:
            p.op("sync", lambda e: e.dma_start(
                yd[(m - 1) * 128:(m + 1) * 128, :].rearrange("(t q) d -> q t d", q=128),
                h[:, m - 1:m + 1, :]),
                reads=[("h", m - 1), ("h", m)], dma=True, cost=3700.0,
                prio_bonus=40000.0)

    prev = None
    for m in range(NSLOT):
        st = ffn_part1(m)
        if prev is not None:
            ffn_part2(prev)
        prev = st
    ffn_part2(prev)

    est = p.emit(nc, sems)
    ex.close()
    return nc, est


def kernel(L0_attn_x, L0_attn_ema_out, L0_attn_logit_decay, L0_attn_log_tau,
           L0_attn_log_blend, L0_attn_logit_blend_seq, L0_ffn_weight,
           L0_ffn_gate_w, L0_ffn_lora_U, L0_ffn_lora_V, **_unused):
    x = np.ascontiguousarray(np.asarray(L0_attn_x, dtype=np.float32))
    W = np.asarray(L0_ffn_weight, dtype=np.float32)
    gw = np.ascontiguousarray(np.asarray(L0_ffn_gate_w, dtype=np.float32))
    U = np.asarray(L0_ffn_lora_U, dtype=np.float32)
    V = np.asarray(L0_ffn_lora_V, dtype=np.float32)
    ema = np.ascontiguousarray(np.asarray(L0_attn_ema_out, dtype=np.float32).reshape(1, D))
    scal = np.array([[np.float32(L0_attn_log_tau), np.float32(L0_attn_log_blend),
                      np.float32(L0_attn_logit_blend_seq),
                      np.float32(L0_attn_logit_decay)]], dtype=np.float32)
    wT = np.ascontiguousarray(W.T)
    uT = np.ascontiguousarray(U.reshape(128, D).T)
    vP = np.ascontiguousarray(V.transpose(0, 2, 1).reshape(128, D))
    ident = np.eye(128, dtype=ml_dtypes.bfloat16)
    ii, jj = np.indices((128, 128))
    maskM = np.where(jj >= ii, MASKVAL, 0.0).astype(ml_dtypes.bfloat16)
    maskO1 = np.where(jj >= ii + 1, MASKVAL, 0.0).astype(ml_dtypes.bfloat16)
    wm0 = np.ones((128, NSLOT), np.float32)
    wm0[0, 0] = 0.0
    wm1 = np.ones((128, NSLOT), np.float32)

    nc, _est = build_program()
    in_maps = []
    for c in range(8):
        b, rho = c // 2, c % 2
        xg = np.ascontiguousarray(
            np.concatenate([x[b, rho::2], x[b, 1 - rho::2]], axis=0))
        in_maps.append({
            "x": xg, "wt": wT, "ut": uT, "vp": vP, "gw": gw, "ema": ema,
            "scal": scal, "maskm": maskM,
            "masko": maskM if rho == 0 else maskO1,
            "wm16": wm0 if rho == 0 else wm1,
            "ident": ident,
        })
    res = run_bass_kernel_spmd(nc, in_maps, core_ids=list(range(8)))
    out = np.empty((4, T, D), dtype=np.float32)
    for c in range(8):
        b, rho = c // 2, c % 2
        out[b, rho::2, :] = res.results[c]["y"]
    return out


if __name__ == "__main__":
    nc, est = build_program()
    n = sum(1 for _ in nc.all_instructions())
    print("built; instructions:", n, " scheduler-estimated span:", est / 1e3, "us")
